# version 02
# baseline (speedup 1.0000x reference)
import math
from contextlib import ExitStack

import numpy as np
import ml_dtypes

import concourse.bass as bass
import concourse.mybir as mybir
from concourse.bass_utils import run_bass_kernel_spmd

F32 = mybir.dt.float32
BF16 = mybir.dt.bfloat16
I32 = mybir.dt.int32
AX = mybir.AxisListType
ALU = mybir.AluOpType
ACT = mybir.ActivationFunctionType

D = 4096
B = 2
S = 4096
NCORE = 8
TPC = 1024
NT = TPC // 128
NKC = D // 128
HEADS = 32
HDIM = 128
SEGS = [(0, 2048, 7.0, 4), (2048, 3072, 31.0, 6), (3072, 4096, 127.0, 8)]
SOFTMAX_SHIFT = 40.0
LN_EPS = 1e-5
NEG_BIG = -3.0e38


class Sem:
    def __init__(self, sem):
        self.sem = sem
        self.n = 0

    def inc(self, instr, by=1):
        instr.then_inc(self.sem, by)
        self.n += by
        return self.n


def core_tiles(c):
    b = c // 4
    return [b * 32 + (c % 4) + 4 * j for j in range(NT)]


def seq_tiles(c):
    return [(c % 4) + 4 * j for j in range(NT)]



def emit_quant(v, src, dst, scr, P=128, segs=None):
    dr = v.drain
    src3 = src[:P, :].rearrange("p (g k) -> p g k", k=32)
    dst3 = dst[:P, :].rearrange("p (g k) -> p g k", k=32)
    chunks = []
    for (lo, hi, maxq, bits) in (segs if segs is not None else SEGS):
        for c in range(lo, hi, 1024):
            chunks.append((c, min(c + 1024, hi), maxq, bits))
    for (lo, hi, maxq, bits) in chunks:
        W = hi - lo
        G = W // 32
        assert W == 1024 and G == 32
        p2 = float(1 << (bits - 1))
        g3 = src3[:, lo // 32:hi // 32, :]
        gmax, s0, t1, eps, u = (scr[k][:P, :G] for k in ("gm", "s0", "t1", "eps", "u"))
        m0, m, ct = (scr[k][:P, :W] for k in ("m0", "m", "ct"))
        rmq = 1.0 / maxq
        v.tensor_reduce(gmax, g3, AX.X, ALU.max, apply_absolute_value=True)
        dr()
        v.tensor_single_scalar(s0, gmax, rmq, ALU.mult)
        dr()
        for _ in range(2):
            v.scalar_tensor_tensor(t1, s0, p2, gmax, ALU.mult, ALU.subtract)
            dr()
            v.tensor_sub(eps, t1, s0)
            dr()
            v.tensor_single_scalar(u, eps, rmq, ALU.mult)
            dr()
            v.tensor_sub(s0, s0, u)
            dr()
        v.tensor_single_scalar(t1, gmax, 0.0, ALU.is_equal)
        dr()
        v.tensor_add(s0, s0, t1)
        dr()
        v.reciprocal(u, s0)
        dr()
        rb = u.unsqueeze(2).broadcast_to([P, G, 32])
        v.tensor_tensor(m0.rearrange("p (g k) -> p g k", k=32), g3, rb, ALU.mult)
        dr()
        v.tensor_single_scalar(m0, m0, 0.5 + p2, ALU.add)
        dr()
        for j in range(bits - 1, -1, -1):
            tj = float(1 << j)
            src_m = m0 if j == bits - 1 else m
            v.tensor_scalar(ct, src_m, tj, tj, ALU.is_ge, ALU.mult)
            dr()
            v.tensor_sub(m, src_m, ct)
            dr()
        v.scalar_tensor_tensor(ct, m0, -p2, m, ALU.add, ALU.subtract)
        dr()
        sb = s0.unsqueeze(2).broadcast_to([P, G, 32])
        v.tensor_tensor(dst3[:, lo // 32:hi // 32, :],
                        ct.rearrange("p (g k) -> p g k", k=32), sb, ALU.mult)
        dr()


QUANT_SCRATCH = [("gm", [128, 32]), ("s0", [128, 32]), ("t1", [128, 32]),
                 ("eps", [128, 32]), ("u", [128, 32]),
                 ("m0", [128, 1024]), ("m", [128, 1024]), ("ct", [128, 1024])]


def alloc_quant_scratch(nc, prefix=""):
    return {k: nc.alloc_sbuf_tensor(prefix + "qs_" + k, sh, F32).ap()
            for k, sh in QUANT_SCRATCH}



def build_l0():
    nc = bass.Bass("TRN2", target_bir_lowering=False)
    ins = {}
    outs = {}
    for m in ("wq", "wk", "wv", "wo"):
        ins[m] = nc.dram_tensor(m, [512, 4096], F32, kind="ExternalInput").ap()
    ins["fq"] = nc.dram_tensor("fq", [512, 4096], I32, kind="ExternalInput").ap()
    ins["fs"] = nc.dram_tensor("fs", [512, 32], F32, kind="ExternalInput").ap()
    ins["fz"] = nc.dram_tensor("fz", [512, 32], F32, kind="ExternalInput").ap()
    ins["ident"] = nc.dram_tensor("ident", [128, 128], BF16, kind="ExternalInput").ap()
    for m in ("wq", "wk", "wv", "wo", "ff"):
        for p in ("hi", "lo"):
            outs[m + p] = nc.dram_tensor(m + "T_" + p, [32, 128, 512], BF16,
                                         kind="ExternalOutput").ap()

    stage = nc.alloc_sbuf_tensor("stage", [128, 4096], F32).ap()
    stage_i = nc.alloc_sbuf_tensor("stage_i", [128, 4096], I32).ap()
    sfs = nc.alloc_sbuf_tensor("sfs", [128, 32], F32).ap()
    sfz = nc.alloc_sbuf_tensor("sfz", [128, 32], F32).ap()
    dq = nc.alloc_sbuf_tensor("dq", [128, 4096], F32).ap()
    hi = nc.alloc_sbuf_tensor("hi", [128, 4096], BF16).ap()
    lo = nc.alloc_sbuf_tensor("lo", [128, 4096], BF16).ap()
    ident = nc.alloc_sbuf_tensor("identt", [128, 128], BF16).ap()
    ev_hi = nc.alloc_sbuf_tensor("ev_hi", [128, 128], BF16).ap()
    ev_lo = nc.alloc_sbuf_tensor("ev_lo", [128, 128], BF16).ap()
    scr = alloc_quant_scratch(nc)

    with ExitStack() as ctx:
        pt_hi = ctx.enter_context(nc.psum_tensor("pt_hi", [128, 128], BF16))
        pt_lo = ctx.enter_context(nc.psum_tensor("pt_lo", [128, 128], BF16))
        ld = Sem(ctx.enter_context(nc.semaphore("ld")))
        ve = Sem(ctx.enter_context(nc.semaphore("ve")))
        pe = Sem(ctx.enter_context(nc.semaphore("pe")))
        ae = Sem(ctx.enter_context(nc.semaphore("ae")))
        st = Sem(ctx.enter_context(nc.semaphore("st")))
        block = ctx.enter_context(nc.Block())

        MATS = [("wq", True), ("wk", True), ("wv", True), ("wo", True), ("ff", False)]

        @block.sync
        def _(sync):
            sync.dma_start(out=ident[:], in_=ins["ident"][:, :]).then_inc(ld.sem, 16)
            ld.n += 16
            sync.dma_start(out=sfs[:], in_=ins["fs"][:128, :]).then_inc(ld.sem, 16)
            ld.n += 16
            sync.dma_start(out=sfz[:], in_=ins["fz"][:128, :]).then_inc(ld.sem, 16)
            ld.n += 16
            for mi, (m, is_attn) in enumerate(MATS):
                for rt in range(4):
                    need = mi * 4 + rt - 1
                    if need >= 0:
                        sync.wait_ge(ve.sem, need + 1)
                    if is_attn:
                        sync.dma_start(out=stage[:], in_=ins[m][rt * 128:(rt + 1) * 128, :]
                                       ).then_inc(ld.sem, 16)
                    else:
                        sync.dma_start(out=stage_i[:], in_=ins["fq"][rt * 128:(rt + 1) * 128, :]
                                       ).then_inc(ld.sem, 16)
                        sync.dma_start(out=sfs[:], in_=ins["fs"][rt * 128:(rt + 1) * 128, :]
                                       ).then_inc(ld.sem, 16)
                        sync.dma_start(out=sfz[:], in_=ins["fz"][rt * 128:(rt + 1) * 128, :]
                                       ).then_inc(ld.sem, 16)

        lcnt = 48
        lcnt_at = {}
        for mi, (m, is_attn) in enumerate(MATS):
            for rt in range(4):
                lcnt += 16 if is_attn else 48
                lcnt_at[(mi, rt)] = lcnt

        @block.vector
        def _(vector):
            for mi, (m, is_attn) in enumerate(MATS):
                for rt in range(4):
                    vector.wait_ge(ld.sem, lcnt_at[(mi, rt)])
                    t_idx = mi * 4 + rt
                    if t_idx >= 1:
                        vector.wait_ge(pe.sem, 32 * t_idx)
                    if is_attn:
                        emit_quant(vector, stage, dq, scr)
                    else:
                        q3 = stage_i[:, :].rearrange("p (g k) -> p g k", k=128)
                        d3 = dq[:, :].rearrange("p (g k) -> p g k", k=128)
                        sb = sfs[:, :].unsqueeze(2).broadcast_to([128, 32, 128])
                        zb = sfz[:, :].unsqueeze(2).broadcast_to([128, 32, 128])
                        vector.tensor_tensor(d3, q3, sb, ALU.mult)
                        vector.drain()
                        vector.tensor_tensor(d3, d3, zb, ALU.add)
                        vector.drain()
                    vector.tensor_copy(hi[:], dq[:])
                    vector.drain()
                    vector.tensor_sub(lo[:], dq[:], hi[:])
                    vector.drain()
                    ve.inc(vector.engine_nop())

        @block.tensor
        def _(tensor):
            for mi, (m, is_attn) in enumerate(MATS):
                for rt in range(4):
                    tensor.wait_ge(ve.sem, mi * 4 + rt + 1)
                    for kc in range(32):
                        done_pairs = (mi * 4 + rt) * 32 + kc
                        if done_pairs >= 1:
                            tensor.wait_ge(ae.sem, done_pairs)
                        tensor.transpose(pt_hi[:], hi[:, kc * 128:(kc + 1) * 128], ident[:])
                        pe.inc(tensor.transpose(pt_lo[:], lo[:, kc * 128:(kc + 1) * 128], ident[:]))

        @block.scalar
        def _(scalar):
            for mi, (m, is_attn) in enumerate(MATS):
                for rt in range(4):
                    for kc in range(32):
                        n_pair = (mi * 4 + rt) * 32 + kc + 1
                        scalar.wait_ge(pe.sem, n_pair)
                        if n_pair > 1:
                            scalar.wait_ge(st.sem, 32 * (n_pair - 1))
                        scalar.copy(ev_hi[:], pt_hi[:])
                        ae_n = scalar.copy(ev_lo[:], pt_lo[:])
                        ae.inc(ae_n)

        @block.gpsimd
        def _(gpsimd):
            for mi, (m, is_attn) in enumerate(MATS):
                for rt in range(4):
                    for kc in range(32):
                        n_pair = (mi * 4 + rt) * 32 + kc + 1
                        gpsimd.wait_ge(ae.sem, n_pair)
                        gpsimd.dma_start(out=outs[m + "hi"][kc, :, rt * 128:(rt + 1) * 128],
                                         in_=ev_hi[:]).then_inc(st.sem, 16)
                        gpsimd.dma_start(out=outs[m + "lo"][kc, :, rt * 128:(rt + 1) * 128],
                                         in_=ev_lo[:]).then_inc(st.sem, 16)
                        st.n += 32
            gpsimd.wait_ge(st.sem, st.n)
    return nc



def emit_floorhalf(v, src, dst, w1, bits, base):
    dr = v.drain
    v.tensor_single_scalar(dst, src, 0.5, ALU.add)
    dr()
    for j in range(bits - 1, -1, -1):
        tj = float(1 << j)
        sm = dst if j == bits - 1 else w1
        v.tensor_scalar(base, sm, tj, tj, ALU.is_ge, ALU.mult)
        dr()
        v.tensor_sub(w1, sm, base)
        dr()
    v.tensor_sub(dst, dst, w1)
    dr()



def build_l1(do_b=True, nproj=3, nnblk=8, c_stop='all', do_a=True):
    nc = bass.Bass("TRN2", target_bir_lowering=False)
    x_in = nc.dram_tensor("x", [TPC, D], F32, kind="ExternalInput").ap()
    pos_in = nc.dram_tensor("pos", [128, TPC], F32, kind="ExternalInput").ap()
    ident_in = nc.dram_tensor("ident", [128, 128], BF16, kind="ExternalInput").ap()
    w_in = {}
    for m in ("q", "k", "v"):
        for p in ("hi", "lo"):
            w_in[m + p] = nc.dram_tensor("w" + m + "_" + p, [8, 32, 128, 512], BF16,
                                         kind="ExternalInput").ap()
    o_out = {}
    for m in ("q", "k"):
        for p in ("hi", "lo"):
            o_out[m + p] = nc.dram_tensor(m + "T_" + p, [32, 128, TPC], BF16,
                                          kind="ExternalOutput").ap()
    for p in ("hi", "lo"):
        o_out["v" + p] = nc.dram_tensor("v_" + p, [32, NT, 128, 128], BF16,
                                        kind="ExternalOutput").ap()

    xs = nc.alloc_sbuf_tensor("xs", [128, D], F32).ap()
    xhi = nc.alloc_sbuf_tensor("xhi", [128, D], BF16).ap()
    xlo = nc.alloc_sbuf_tensor("xlo", [128, D], BF16).ap()
    xqhi = nc.alloc_sbuf_tensor("xqhi", [128, NKC * TPC], BF16).ap()
    xqlo = nc.alloc_sbuf_tensor("xqlo", [128, NKC * TPC], BF16).ap()
    ident = nc.alloc_sbuf_tensor("identt", [128, 128], BF16).ap()
    scr = alloc_quant_scratch(nc)
    pos_rep = nc.alloc_sbuf_tensor("pos_rep", [128, TPC], F32).ap()
    ones64 = nc.alloc_sbuf_tensor("ones64", [1, 64], F32).ap()
    cos_tab = nc.alloc_sbuf_tensor("cos_tab", [128, TPC], F32).ap()
    sin_tab = nc.alloc_sbuf_tensor("sin_tab", [128, TPC], F32).ap()
    big2 = [nc.alloc_sbuf_tensor(f"big{i}", [128, TPC], F32).ap() for i in range(2)]
    c_sc = nc.alloc_sbuf_tensor("c_sc", [1, 1], F32).ap()
    c_ln = nc.alloc_sbuf_tensor("c_ln", [1, 1], F32).ap()
    c_rep = nc.alloc_sbuf_tensor("c_rep", [64, 1], F32).ap()
    iot = nc.alloc_sbuf_tensor("iot", [64, 1], F32).ap()
    iexp = nc.alloc_sbuf_tensor("iexp", [64, 1], F32).ap()
    inv_t = nc.alloc_sbuf_tensor("inv_t", [128, 1], F32).ap()
    wbuf = {(p, b): nc.alloc_sbuf_tensor(f"wbuf_{p}{b}", [128, 512], BF16).ap()
            for p in ("hi", "lo") for b in range(2)}
    stage = [big2[0][:, 0:512], big2[0][:, 512:1024],
             big2[1][:, 0:512], big2[1][:, 512:1024]]
    ro = [scr["m0"][:, 0:512], scr["m0"][:, 512:1024],
          scr["m"][:, 0:512], scr["m"][:, 512:1024]]
    rt1 = nc.alloc_sbuf_tensor("rt1", [128, 512], F32).ap()
    rt2 = nc.alloc_sbuf_tensor("rt2", [128, 512], F32).ap()
    ang, tw3 = big2[0], big2[1]
    tw1, tw2 = scr["m0"][:, :], scr["m"][:, :]
    shi = [xhi[:, i * 512:(i + 1) * 512] for i in range(4)]
    slo = [xlo[:, i * 512:(i + 1) * 512] for i in range(4)]
    vstg = xhi[:, 2048:2560]
    vstg2 = xhi[:, 2560:3072]

    TWO_PI = 2.0 * math.pi
    C1 = float(np.float32(TWO_PI))
    C2 = float(np.float64(TWO_PI) - np.float64(np.float32(TWO_PI)))
    PI_CLAMP = 3.1415925

    with ExitStack() as ctx:
        psum = [ctx.enter_context(nc.psum_tensor(f"pj{i}", [128, 512], F32))
                for i in range(4)]
        pt = ctx.enter_context(nc.psum_tensor("pt", [128, 512], BF16))
        ptf = ctx.enter_context(nc.psum_tensor("ptf", [128, 512], F32))
        ld = Sem(ctx.enter_context(nc.semaphore("ld")))
        wl = Sem(ctx.enter_context(nc.semaphore("wl")))
        ve = Sem(ctx.enter_context(nc.semaphore("ve")))
        pe = Sem(ctx.enter_context(nc.semaphore("pe")))
        pw = Sem(ctx.enter_context(nc.semaphore("pw")))
        ae = Sem(ctx.enter_context(nc.semaphore("ae")))
        ge = Sem(ctx.enter_context(nc.semaphore("ge")))
        st = Sem(ctx.enter_context(nc.semaphore("st")))
        block = ctx.enter_context(nc.Block())

        @block.sync
        def _(s):
            ld.inc(s.dma_start(out=ident[:], in_=ident_in[:, :]), 16)
            ld.inc(s.dma_start(out=pos_rep[:], in_=pos_in[:, :]), 16)

        if not do_a:
            pass

        if do_a:
          @block.gpsimd
          def _(g):
            g.wait_ge(ld.sem, ld.n)
            g.memset(c_sc[:, :], 10000.0)
            g.memset(ones64[:, :], 1.0)
            g.iota(iot[:, :], [[0, 1]], base=0, channel_multiplier=1,
                   allow_small_or_imprecise_dtypes=True)
            ge.inc(g.engine_nop())

          @block.scalar
          def _(a):
              a.wait_ge(ge.sem, ge.n)
              ae.inc(a.activation(c_ln[:, :], c_sc[:, :], ACT.Ln))

          @block.vector
          def _(v):
              v.wait_ge(ae.sem, ae.n)
              v.tensor_single_scalar(c_ln[:, :], c_ln[:, :], -1.0 / 64.0, ALU.mult)
              ve.inc(v.drain())

          @block.tensor
          def _(t):
              t.wait_ge(ve.sem, ve.n)
              pe.inc(t.matmul(psum[0][0:64, 0:1], ones64[0:1, :], c_ln[0:1, :],
                              start=True, stop=True))

          @block.scalar
          def _(a):
              a.wait_ge(pe.sem, pe.n)
              ae.inc(a.copy(c_rep[:, :], psum[0][0:64, 0:1]))

          @block.vector
          def _(v):
              v.wait_ge(ae.sem, ae.n)
              v.tensor_tensor(iexp[:, :], iot[:, :], c_rep[:, :], ALU.mult)
              ve.inc(v.drain())

          @block.scalar
          def _(a):
              a.wait_ge(ve.sem, ve.n)
              ae.inc(a.activation(inv_t[0:64, :], iexp[:, :], ACT.Exp))

          @block.sync
          def _(s):
              s.wait_ge(ae.sem, ae.n)
              ld.inc(s.dma_start(out=inv_t[64:128, :], in_=inv_t[0:64, :]), 16)

          for trig in ("sin", "cos"):
              @block.vector
              def _(v, trig=trig):
                  v.wait_ge(ld.sem, ld.n)
                  v.wait_ge(ae.sem, ae.n)
                  if trig == "sin":
                      v.tensor_single_scalar(ang[:, :], pos_rep[:, :], inv_t[:, 0:1],
                                             ALU.mult)
                  else:
                      v.tensor_scalar(ang[:, :], pos_rep[:, :], inv_t[:, 0:1],
                                      float(np.float32(math.pi / 2)), ALU.mult, ALU.add)
                  v.drain()
                  v.tensor_single_scalar(tw3[:, :], ang[:, :], 1.0 / TWO_PI, ALU.mult)
                  v.drain()
                  emit_floorhalf(v, tw3, tw1, tw2, 11, scr["ct"][:, :TPC])
                  v.scalar_tensor_tensor(tw2[:, :], tw1[:, :], -C1, ang[:, :],
                                         ALU.mult, ALU.add)
                  v.drain()
                  v.scalar_tensor_tensor(tw3[:, :], tw1[:, :], -C2, tw2[:, :],
                                         ALU.mult, ALU.add)
                  v.drain()
                  v.tensor_scalar(tw3[:, :], tw3[:, :], -PI_CLAMP, PI_CLAMP,
                                  ALU.max, ALU.min)
                  ve.inc(v.drain())

              @block.scalar
              def _(a, trig=trig):
                  a.wait_ge(ve.sem, ve.n)
                  dst = sin_tab if trig == "sin" else cos_tab
                  ae.inc(a.activation(dst[:, :], tw3[:, :], ACT.Sin))


        for tt in range(NT if do_b else 0):
            @block.sync
            def _(s, tt=tt):
                s.wait_ge(ve.sem, ve.n)
                ld.inc(s.dma_start(out=xs[:], in_=x_in[tt * 128:(tt + 1) * 128, :]), 16)

            @block.vector
            def _(v, tt=tt):
                v.wait_ge(ld.sem, ld.n)
                v.wait_ge(ae.sem, ae.n)
                emit_quant(v, xs, xs, scr)
                v.tensor_copy(xhi[:], xs[:])
                v.drain()
                v.tensor_sub(xlo[:], xs[:], xhi[:])
                ve.inc(v.drain())

            for kcg in range(8):
                for part, buf, dstq in (("hi", xhi, xqhi), ("lo", xlo, xqlo)):
                    @block.tensor
                    def _(t, kcg=kcg, buf=buf):
                        t.wait_ge(ve.sem, ve.n)
                        t.wait_ge(ae.sem, ae.n)
                        for i in range(4):
                            kc = kcg * 4 + i
                            tr = t.transpose(pt[:, i * 128:(i + 1) * 128],
                                             buf[:, kc * 128:(kc + 1) * 128], ident[:])
                        pe.inc(tr)

                    @block.scalar
                    def _(a, tt=tt, kcg=kcg, dstq=dstq):
                        a.wait_ge(pe.sem, pe.n)
                        dst3 = dstq[:, :].rearrange("p (kc t) -> p kc t", t=TPC)
                        dst_sl = dst3[:, kcg * 4:(kcg + 1) * 4, tt * 128:(tt + 1) * 128]
                        ae.inc(a.copy(dst_sl,
                                      pt[:, :].rearrange("p (i q) -> p i q", q=128)))

        for pi, proj in enumerate(("q", "k", "v")[:nproj]):
            whi_d, wlo_d = w_in[proj + "hi"], w_in[proj + "lo"]
            for tc in range(2):
                for nbi in range(nnblk):
                    @block.sync
                    def _(s, pi=pi, tc=tc, nbi=nbi, whi_d=whi_d, wlo_d=wlo_d):
                        for kc in range(32):
                            slab = ((pi * 2 + tc) * 8 + nbi) * 32 + kc
                            if slab >= 2:
                                s.wait_ge(pw.sem, slab - 1)
                            b = slab % 2
                            wl.inc(s.dma_start(out=wbuf[("hi", b)][:],
                                               in_=whi_d[nbi, kc, :, :]), 16)
                            wl.inc(s.dma_start(out=wbuf[("lo", b)][:],
                                               in_=wlo_d[nbi, kc, :, :]), 16)

                    @block.tensor
                    def _(t, pi=pi, tc=tc, nbi=nbi):
                        t.wait_ge(ae.sem, ae.n)
                        for kc in range(32):
                            slab = ((pi * 2 + tc) * 8 + nbi) * 32 + kc
                            t.wait_ge(wl.sem, 32 * (slab + 1))
                            b = slab % 2
                            first = kc == 0
                            last = kc == 31
                            for nc4 in range(4):
                                mv = xqhi[:, kc * TPC + tc * 512: kc * TPC + tc * 512 + 512]
                                mvl = xqlo[:, kc * TPC + tc * 512: kc * TPC + tc * 512 + 512]
                                wh = wbuf[("hi", b)][:, nc4 * 128:(nc4 + 1) * 128]
                                wll = wbuf[("lo", b)][:, nc4 * 128:(nc4 + 1) * 128]
                                t.matmul(psum[nc4][:], wh, mv, start=first, stop=False)
                                t.matmul(psum[nc4][:], wh, mvl, start=False, stop=False)
                                mm = t.matmul(psum[nc4][:], wll, mv, start=False, stop=last)
                            pw.inc(mm)

                    if c_stop == 'mm':
                        continue

                    @block.scalar
                    def _(a):
                        a.wait_ge(pw.sem, pw.n)
                        a.wait_ge(ve.sem, ve.n)
                        for nc4 in range(4):
                            ae.inc(a.copy(stage[nc4][:], psum[nc4][:]))
                    if c_stop == 'evac':
                        continue

                    if proj in ("q", "k"):
                        @block.vector
                        def _(v, tc=tc):
                            v.wait_ge(ae.sem, ae.n)
                            v.wait_ge(st.sem, st.n)
                            cs = cos_tab[:, tc * 512:(tc + 1) * 512]
                            sn = sin_tab[:, tc * 512:(tc + 1) * 512]
                            for pr in range(2):
                                A, Bc = stage[pr], stage[2 + pr]
                                oA, oB = ro[pr], ro[2 + pr]
                                v.tensor_tensor(rt1[:], A[:], cs, ALU.mult)
                                v.tensor_tensor(rt2[:], Bc[:], sn, ALU.mult)
                                v.drain()
                                v.tensor_sub(oA[:], rt1[:], rt2[:])
                                v.tensor_tensor(rt1[:], Bc[:], cs, ALU.mult)
                                v.tensor_tensor(rt2[:], A[:], sn, ALU.mult)
                                v.drain()
                                v.tensor_add(oB[:], rt1[:], rt2[:])
                                v.drain()
                            for i in range(4):
                                v.tensor_copy(shi[i][:], ro[i][:])
                            v.drain()
                            for i in range(4):
                                v.tensor_sub(slo[i][:], ro[i][:], shi[i][:])
                            ve.inc(v.drain())

                        @block.gpsimd
                        def _(g, proj=proj, tc=tc, nbi=nbi):
                            g.wait_ge(ve.sem, ve.n)
                            dh, dl = o_out[proj + "hi"], o_out[proj + "lo"]
                            tsl = slice(tc * 512, tc * 512 + 512)
                            for pr in range(2):
                                for hh in range(2):
                                    h = 4 * nbi + 2 * pr + hh
                                    sl = slice(hh * 64, hh * 64 + 64)
                                    st.inc(g.dma_start(out=dh[h, 0:64, tsl],
                                                       in_=shi[pr][sl, :]), 16)
                                    st.inc(g.dma_start(out=dh[h, 64:128, tsl],
                                                       in_=shi[2 + pr][sl, :]), 16)
                                    st.inc(g.dma_start(out=dl[h, 0:64, tsl],
                                                       in_=slo[pr][sl, :]), 16)
                                    st.inc(g.dma_start(out=dl[h, 64:128, tsl],
                                                       in_=slo[2 + pr][sl, :]), 16)
                    else:
                        for nc4 in range(4):
                            @block.vector
                            def _(v, nc4=nc4):
                                v.wait_ge(ae.sem, ae.n)
                                if nc4 == 0:
                                    v.wait_ge(st.sem, st.n)
                                v.wait_ge(pe.sem, pe.n)
                                v.tensor_copy(shi[0][:], stage[nc4][:])
                                v.drain()
                                v.tensor_sub(slo[0][:], stage[nc4][:], shi[0][:])
                                ve.inc(v.drain())

                            for part, sbuf_, vdst in (("hi", shi, vstg), ("lo", slo, vstg2)):
                                @block.tensor
                                def _(t, sbuf_=sbuf_):
                                    t.wait_ge(ve.sem, ve.n)
                                    t.wait_ge(ae.sem, ae.n)
                                    for i in range(4):
                                        tr = t.transpose(pt[:, i * 128:(i + 1) * 128],
                                                         sbuf_[0][:, i * 128:(i + 1) * 128],
                                                         ident[:])
                                    pe.inc(tr)

                                @block.scalar
                                def _(a, vdst=vdst):
                                    a.wait_ge(pe.sem, pe.n)
                                    a.wait_ge(st.sem, st.n)
                                    ae.inc(a.copy(vdst[:], pt[:]))

                            @block.gpsimd
                            def _(g, tc=tc, nbi=nbi, nc4=nc4):
                                g.wait_ge(ae.sem, ae.n)
                                h = 4 * nbi + nc4
                                for i in range(4):
                                    ttile = tc * 4 + i
                                    st.inc(g.dma_start(
                                        out=o_out["vhi"][h, ttile, :, :],
                                        in_=vstg[:, i * 128:(i + 1) * 128]), 16)
                                    st.inc(g.dma_start(
                                        out=o_out["vlo"][h, ttile, :, :],
                                        in_=vstg2[:, i * 128:(i + 1) * 128]), 16)

        @block.gpsimd
        def _(g):
            g.wait_ge(st.sem, st.n)
    return nc



def host_ab_perm():
    idx = np.arange(D).reshape(HEADS, 2, 64)
    return np.concatenate([idx[:, 0, :].ravel(), idx[:, 1, :].ravel()])


def host_block_w(wT_hi, wT_lo, ab):
    out = []
    for t in (wT_hi, wT_lo):
        a = t.reshape(32, 128, 32, 128)
        blocks = []
        for j in range(8):
            chunks = [2 * j, 2 * j + 1, 16 + 2 * j, 16 + 2 * j + 1] if ab else \
                     [4 * j, 4 * j + 1, 4 * j + 2, 4 * j + 3]
            b = a[:, :, chunks, :]
            blocks.append(b.reshape(32, 128, 512))
        out.append(np.ascontiguousarray(np.stack(blocks, axis=0)))
    return out[0], out[1]


def host_positions(c):
    ps = []
    for p in seq_tiles(c):
        ps.extend(range(p * 128, p * 128 + 128))
    return np.asarray(ps, dtype=np.float32).reshape(1, TPC)



def l2_jmin(kc):
    return max(0, -(-(kc - 3) // 4))


def build_l2():
    nc = bass.Bass("TRN2", target_bir_lowering=False)
    ins = {}
    for m in ("q", "k"):
        for p in ("hi", "lo"):
            ins[m + p] = nc.dram_tensor(m + "T_" + p, [32, 128, S if m == "k" else TPC],
                                        BF16, kind="ExternalInput").ap()
    for p in ("hi", "lo"):
        ins["v" + p] = nc.dram_tensor("v_" + p, [32, 32, 128, 128], BF16,
                                      kind="ExternalInput").ap()
    masks_in = nc.dram_tensor("masks", [32, 128, 128], F32, kind="ExternalInput").ap()
    ident_in = nc.dram_tensor("ident", [128, 128], BF16, kind="ExternalInput").ap()
    identf_in = nc.dram_tensor("identf", [128, 128], F32, kind="ExternalInput").ap()
    o_out = nc.dram_tensor("o", [TPC, D], F32, kind="ExternalOutput").ap()

    sq = {p: nc.alloc_sbuf_tensor(f"sq_{p}", [128, TPC], BF16).ap() for p in ("hi", "lo")}
    sk = {p: nc.alloc_sbuf_tensor(f"sk_{p}", [128, S], BF16).ap() for p in ("hi", "lo")}
    sv = {p: nc.alloc_sbuf_tensor(f"sv_{p}", [128, 32 * 128], BF16).ap() for p in ("hi", "lo")}
    smask = nc.alloc_sbuf_tensor("smask", [128, 32 * 128], F32).ap()
    ident = nc.alloc_sbuf_tensor("identt", [128, 128], BF16).ap()
    identf = nc.alloc_sbuf_tensor("identtf", [128, 128], F32).ap()
    onesk = nc.alloc_sbuf_tensor("onesk", [128, 1], BF16).ap()
    zerok = nc.alloc_sbuf_tensor("zerok", [128, 1], BF16).ap()
    pf = nc.alloc_sbuf_tensor("pf", [128, TPC], F32).ap()
    phi = nc.alloc_sbuf_tensor("phi", [128, TPC], BF16).ap()
    plo = nc.alloc_sbuf_tensor("plo", [128, TPC], BF16).ap()
    osb = nc.alloc_sbuf_tensor("osb", [128, TPC], F32).ap()
    zr = nc.alloc_sbuf_tensor("zr", [1, TPC], F32).ap()
    zrs = nc.alloc_sbuf_tensor("zrs", [128, NT], F32).ap()
    otm = nc.alloc_sbuf_tensor("otm", [128, NT * 128], F32).ap()
    expb = nc.alloc_sbuf_tensor("expb", [128, 1], F32).ap()
    exps = nc.alloc_sbuf_tensor("exps", [128, 1], F32).ap()

    SCALE = 1.0 / math.sqrt(128.0)

    with ExitStack() as ctx:
        pS = [ctx.enter_context(nc.psum_tensor(f"pS{i}", [128, 512], F32)) for i in range(2)]
        pO = [ctx.enter_context(nc.psum_tensor(f"pO{i}", [128, 512], F32)) for i in range(2)]
        pZ = [ctx.enter_context(nc.psum_tensor(f"pZ{i}", [1, 512], F32)) for i in range(2)]
        pzt = ctx.enter_context(nc.psum_tensor("pzt", [128, NT], F32))
        ptm = ctx.enter_context(nc.psum_tensor("ptm", [128, 512], F32))
        ld = Sem(ctx.enter_context(nc.semaphore("ld")))
        ve = Sem(ctx.enter_context(nc.semaphore("ve")))
        pe = Sem(ctx.enter_context(nc.semaphore("pe")))
        ae = Sem(ctx.enter_context(nc.semaphore("ae")))
        ge = Sem(ctx.enter_context(nc.semaphore("ge")))
        st = Sem(ctx.enter_context(nc.semaphore("st")))
        block = ctx.enter_context(nc.Block())

        @block.sync
        def _(s):
            ld.inc(s.dma_start(out=ident[:], in_=ident_in[:, :]), 16)
            ld.inc(s.dma_start(out=identf[:], in_=identf_in[:, :]), 16)
            ld.inc(s.dma_start(
                out=smask[:, :].rearrange("p (kc q) -> p kc q", q=128),
                in_=masks_in[:, :, :].transpose([1, 0, 2])), 16)

        @block.gpsimd
        def _(g):
            g.memset(onesk[:, :], 1.0)
            g.memset(zerok[:, :], 0.0)
            g.memset(expb[:, :], -SOFTMAX_SHIFT)
            g.memset(exps[:, :], SCALE)
            ge.inc(g.engine_nop())

        for h in range(HEADS):
            @block.sync
            def _(s, h=h):
                s.wait_ge(ve.sem, ve.n)
                s.wait_ge(st.sem, st.n)
                for p in ("hi", "lo"):
                    ld.inc(s.dma_start(out=sq[p][:], in_=ins["q" + p][h, :, :]), 16)
                    ld.inc(s.dma_start(out=sk[p][:], in_=ins["k" + p][h, :, :]), 16)
                    ld.inc(s.dma_start(
                        out=sv[p][:, :].rearrange("t (kc d) -> t kc d", d=128),
                        in_=ins["v" + p][h, :, :, :].transpose([1, 0, 2])), 16)

            for kc in range(32):
                q0 = l2_jmin(kc) * 128
                chunks = []
                for c in range(2):
                    lo_c = max(q0, c * 512)
                    hi_c = (c + 1) * 512
                    if lo_c < hi_c:
                        chunks.append((c, lo_c, hi_c))

                @block.tensor
                def _(t, h=h, kc=kc, q0=q0, chunks=chunks):
                    t.wait_ge(ld.sem, ld.n)
                    if kc > 0:
                        t.wait_ge(ae.sem, ae.n)
                    kh = sk["hi"][:, kc * 128:(kc + 1) * 128]
                    kl = sk["lo"][:, kc * 128:(kc + 1) * 128]
                    for (c, lo_c, hi_c) in chunks:
                        r = slice(lo_c - c * 512, hi_c - c * 512)
                        qh = sq["hi"][:, lo_c:hi_c]
                        ql = sq["lo"][:, lo_c:hi_c]
                        t.matmul(pS[c][:, r], kh, qh, start=True, stop=False)
                        t.matmul(pS[c][:, r], kh, ql, start=False, stop=False)
                        mm = t.matmul(pS[c][:, r], kl, qh, start=False, stop=True)
                    pe.inc(mm)

                @block.vector
                def _(v, kc=kc, q0=q0):
                    v.wait_ge(pe.sem, pe.n)
                    v.wait_ge(ge.sem, ge.n)
                    c0 = q0 // 512
                    r = slice(q0 - c0 * 512, q0 - c0 * 512 + 128)
                    msl = smask[:, kc * 128:(kc + 1) * 128]
                    v.tensor_tensor(pS[c0][:, r], pS[c0][:, r], msl, ALU.add)
                    ve.inc(v.drain())

                @block.scalar
                def _(a, kc=kc, chunks=chunks):
                    a.wait_ge(ve.sem, ve.n)
                    for (c, lo_c, hi_c) in chunks:
                        r = slice(lo_c - c * 512, hi_c - c * 512)
                        inst = a.activation(pf[:, lo_c:hi_c], pS[c][:, r], ACT.Exp,
                                            bias=expb[:, 0:1], scale=exps[:, 0:1])
                    ae.inc(inst)

                @block.vector
                def _(v, kc=kc, q0=q0):
                    v.wait_ge(ae.sem, ae.n)
                    v.tensor_copy(phi[:, q0:TPC], pf[:, q0:TPC])
                    v.drain()
                    v.tensor_sub(plo[:, q0:TPC], pf[:, q0:TPC], phi[:, q0:TPC])
                    ve.inc(v.drain())

                @block.tensor
                def _(t, h=h, kc=kc, chunks=chunks):
                    t.wait_ge(ve.sem, ve.n)
                    vh = sv["hi"][:, kc * 128:(kc + 1) * 128]
                    vl = sv["lo"][:, kc * 128:(kc + 1) * 128]
                    first = kc == 0
                    for (c, lo_c, hi_c) in chunks:
                        r = slice(lo_c - c * 512, hi_c - c * 512)
                        ph_ = phi[:, lo_c:hi_c]
                        pl_ = plo[:, lo_c:hi_c]
                        t.matmul(pO[c][:, r], vh, ph_, start=first, stop=False)
                        t.matmul(pO[c][:, r], vh, pl_, start=False, stop=False)
                        t.matmul(pO[c][:, r], vl, ph_, start=False, stop=False)
                        t.matmul(pZ[c][:, r], onesk[:, :], ph_, start=first, stop=False)
                        mm = t.matmul(pZ[c][:, r], onesk[:, :], pl_, start=False,
                                      stop=False)
                    pe.inc(mm)

            @block.tensor
            def _(t, h=h):
                t.matmul(pO[0][0:1, 0:1], zerok[:, 0:1], zerok[:, 0:1],
                         start=False, stop=True)
                t.matmul(pO[1][0:1, 0:1], zerok[:, 0:1], zerok[:, 0:1],
                         start=False, stop=True)
                t.matmul(pZ[0][0:1, 0:1], zerok[:, 0:1], zerok[:, 0:1],
                         start=False, stop=True)
                mm = t.matmul(pZ[1][0:1, 0:1], zerok[:, 0:1], zerok[:, 0:1],
                              start=False, stop=True)
                pe.inc(mm)

            @block.vector
            def _(v):
                v.wait_ge(pe.sem, pe.n)
                v.reciprocal(zr[0:1, 0:512], pZ[0][0:1, :])
                v.reciprocal(zr[0:1, 512:1024], pZ[1][0:1, :])
                ve.inc(v.drain())

            @block.scalar
            def _(a):
                a.wait_ge(pe.sem, pe.n)
                a.copy(osb[:, 0:512], pO[0][:, :])
                ae.inc(a.copy(osb[:, 512:1024], pO[1][:, :]))

            @block.tensor
            def _(t):
                t.wait_ge(ve.sem, ve.n)
                for tt in range(NT):
                    tr = t.transpose(pzt[:, tt:tt + 1], zr[0:1, tt * 128:(tt + 1) * 128],
                                     identf[0:1, 0:1])
                pe.inc(tr)

            @block.scalar
            def _(a):
                a.wait_ge(pe.sem, pe.n)
                ae.inc(a.copy(zrs[:, :], pzt[:, :]))

            for half in range(2):
                @block.tensor
                def _(t, half=half):
                    t.wait_ge(ae.sem, ae.n)
                    for i in range(4):
                        tt = half * 4 + i
                        tr = t.transpose(ptm[:, i * 128:(i + 1) * 128],
                                         osb[:, tt * 128:(tt + 1) * 128], identf[:])
                    pe.inc(tr)

                @block.scalar
                def _(a, half=half):
                    a.wait_ge(pe.sem, pe.n)
                    ae.inc(a.copy(otm[:, half * 512:(half + 1) * 512], ptm[:, :]))

            @block.vector
            def _(v):
                v.wait_ge(ae.sem, ae.n)
                for tt in range(NT):
                    v.tensor_single_scalar(otm[:, tt * 128:(tt + 1) * 128],
                                           otm[:, tt * 128:(tt + 1) * 128],
                                           zrs[:, tt:tt + 1], ALU.mult)
                ve.inc(v.drain())

            @block.gpsimd
            def _(g, h=h):
                g.wait_ge(ve.sem, ve.n)
                for tt in range(NT):
                    st.inc(g.dma_start(
                        out=o_out[tt * 128:(tt + 1) * 128, h * 128:(h + 1) * 128],
                        in_=otm[:, tt * 128:(tt + 1) * 128]), 16)

        @block.gpsimd
        def _(g):
            g.wait_ge(st.sem, st.n)
    return nc


def host_masks(c):
    out = np.zeros((32, 128, 128), dtype=np.float32)
    tiles = seq_tiles(c)
    for kc in range(32):
        p = tiles[l2_jmin(kc)]
        if p > kc:
            continue
        kg = kc * 128 + np.arange(128)[:, None]
        qg = p * 128 + np.arange(128)[None, :]
        out[kc] = np.where(qg >= kg, 0.0, NEG_BIG).astype(np.float32)
    return out



def build_l3(dbg=False):
    nc = bass.Bass("TRN2", target_bir_lowering=False)
    o_in = nc.dram_tensor("o", [TPC, D], F32, kind="ExternalInput").ap()
    x_in = nc.dram_tensor("x", [TPC, D], F32, kind="ExternalInput").ap()
    ident_in = nc.dram_tensor("ident", [128, 128], BF16, kind="ExternalInput").ap()
    identf_in = nc.dram_tensor("identf", [128, 128], F32, kind="ExternalInput").ap()
    bias_in = nc.dram_tensor("bias", [128, 32], F32, kind="ExternalInput").ap()
    w_in = {}
    for m in ("wo", "ff"):
        for p in ("hi", "lo"):
            w_in[m + p] = nc.dram_tensor(m + "_" + p, [8, 32, 128, 512], BF16,
                                         kind="ExternalInput").ap()
    y_out = nc.dram_tensor("y", [TPC, D], F32, kind="ExternalOutput").ap()
    knd = "ExternalOutput" if dbg else "Internal"
    aT = nc.dram_tensor("aT", [TPC, D], F32, kind=knd).ap()
    fT = nc.dram_tensor("fT", [TPC, D], F32, kind=knd).ap()
    x1r = nc.dram_tensor("x1r", [TPC, D], F32, kind=knd).ap()

    xs = nc.alloc_sbuf_tensor("xs", [128, D], F32).ap()
    xt2 = nc.alloc_sbuf_tensor("xt2", [128, D], F32).ap()
    xhi = nc.alloc_sbuf_tensor("xhi", [128, D], BF16).ap()
    xlo = nc.alloc_sbuf_tensor("xlo", [128, D], BF16).ap()
    xqhi = nc.alloc_sbuf_tensor("xqhi", [128, NKC * TPC], BF16).ap()
    xqlo = nc.alloc_sbuf_tensor("xqlo", [128, NKC * TPC], BF16).ap()
    ident = nc.alloc_sbuf_tensor("identt", [128, 128], BF16).ap()
    identf = nc.alloc_sbuf_tensor("identtf", [128, 128], F32).ap()
    sbias = nc.alloc_sbuf_tensor("sbias", [128, 32], F32).ap()
    stage2 = nc.alloc_sbuf_tensor("stage2", [128, 512], F32).ap()
    scr = alloc_quant_scratch(nc)
    wbuf = {(p, b): nc.alloc_sbuf_tensor(f"wbuf_{p}{b}", [128, 512], BF16).ap()
            for p in ("hi", "lo") for b in range(2)}
    stage = [nc.alloc_sbuf_tensor(f"stg{i}", [128, 512], F32).ap() for i in range(4)]
    lm = nc.alloc_sbuf_tensor("lm", [128, 1], F32).ap()
    lv = nc.alloc_sbuf_tensor("lv", [128, 1], F32).ap()
    lsd = nc.alloc_sbuf_tensor("lsd", [128, 1], F32).ap()
    lrs = nc.alloc_sbuf_tensor("lrs", [128, 1], F32).ap()
    ln_s = nc.alloc_sbuf_tensor("ln_s", [128, 1], F32).ap()
    ln_b = nc.alloc_sbuf_tensor("ln_b", [128, 1], F32).ap()

    with ExitStack() as ctx:
        psum = [ctx.enter_context(nc.psum_tensor(f"pj{i}", [128, 512], F32))
                for i in range(4)]
        pt = ctx.enter_context(nc.psum_tensor("pt", [128, 512], BF16))
        ptf = ctx.enter_context(nc.psum_tensor("ptf", [128, 512], F32))
        ld = Sem(ctx.enter_context(nc.semaphore("ld")))
        wl = Sem(ctx.enter_context(nc.semaphore("wl")))
        ve = Sem(ctx.enter_context(nc.semaphore("ve")))
        pe = Sem(ctx.enter_context(nc.semaphore("pe")))
        pw = Sem(ctx.enter_context(nc.semaphore("pw")))
        ae = Sem(ctx.enter_context(nc.semaphore("ae")))
        ge = Sem(ctx.enter_context(nc.semaphore("ge")))
        st = Sem(ctx.enter_context(nc.semaphore("st")))
        block = ctx.enter_context(nc.Block())

        @block.sync
        def _(s):
            ld.inc(s.dma_start(out=ident[:], in_=ident_in[:, :]), 16)
            ld.inc(s.dma_start(out=identf[:], in_=identf_in[:, :]), 16)
            ld.inc(s.dma_start(out=sbias[:], in_=bias_in[:, :]), 16)

        @block.gpsimd
        def _(g):
            g.memset(ln_s[:, :], 1.0 / 4096.0)
            g.memset(ln_b[:, :], LN_EPS)
            ge.inc(g.engine_nop())

        def quant_split_transpose(src_dram, row_base=0):
            for tt in range(NT):
                @block.sync
                def _(s, tt=tt):
                    s.wait_ge(ve.sem, ve.n)
                    ld.inc(s.dma_start(out=xs[:],
                                       in_=src_dram[tt * 128:(tt + 1) * 128, :]), 16)

                @block.vector
                def _(v, tt=tt):
                    v.wait_ge(ld.sem, ld.n)
                    v.wait_ge(ae.sem, ae.n)
                    emit_quant(v, xs, xs, scr)
                    v.tensor_copy(xhi[:], xs[:])
                    v.drain()
                    v.tensor_sub(xlo[:], xs[:], xhi[:])
                    ve.inc(v.drain())

                for kcg in range(8):
                    for part, buf, dstq in (("hi", xhi, xqhi), ("lo", xlo, xqlo)):
                        @block.tensor
                        def _(t, kcg=kcg, buf=buf):
                            t.wait_ge(ve.sem, ve.n)
                            t.wait_ge(ae.sem, ae.n)
                            for i in range(4):
                                kc = kcg * 4 + i
                                tr = t.transpose(pt[:, i * 128:(i + 1) * 128],
                                                 buf[:, kc * 128:(kc + 1) * 128],
                                                 ident[:])
                            pe.inc(tr)

                        @block.scalar
                        def _(a, tt=tt, kcg=kcg, dstq=dstq):
                            a.wait_ge(pe.sem, pe.n)
                            dst3 = dstq[:, :].rearrange("p (kc t) -> p kc t", t=TPC)
                            dst_sl = dst3[:, kcg * 4:(kcg + 1) * 4,
                                          tt * 128:(tt + 1) * 128]
                            ae.inc(a.copy(dst_sl,
                                          pt[:, :].rearrange("p (i q) -> p i q", q=128)))

        def projection(pi, whi_d, wlo_d, dst_dram, with_bias):
            for tc in range(2):
                for nbi in range(8):
                    @block.sync
                    def _(s, pi=pi, tc=tc, nbi=nbi, whi_d=whi_d, wlo_d=wlo_d):
                        for kc in range(32):
                            slab = ((pi * 2 + tc) * 8 + nbi) * 32 + kc
                            if slab >= 2:
                                s.wait_ge(pw.sem, slab - 1)
                            b = slab % 2
                            wl.inc(s.dma_start(out=wbuf[("hi", b)][:],
                                               in_=whi_d[nbi, kc, :, :]), 16)
                            wl.inc(s.dma_start(out=wbuf[("lo", b)][:],
                                               in_=wlo_d[nbi, kc, :, :]), 16)

                    @block.tensor
                    def _(t, pi=pi, tc=tc, nbi=nbi):
                        t.wait_ge(ae.sem, ae.n)
                        for kc in range(32):
                            slab = ((pi * 2 + tc) * 8 + nbi) * 32 + kc
                            t.wait_ge(wl.sem, 32 * (slab + 1))
                            b = slab % 2
                            first = kc == 0
                            last = kc == 31
                            for nc4 in range(4):
                                mv = xqhi[:, kc * TPC + tc * 512:
                                          kc * TPC + tc * 512 + 512]
                                mvl = xqlo[:, kc * TPC + tc * 512:
                                           kc * TPC + tc * 512 + 512]
                                wh = wbuf[("hi", b)][:, nc4 * 128:(nc4 + 1) * 128]
                                wll = wbuf[("lo", b)][:, nc4 * 128:(nc4 + 1) * 128]
                                t.matmul(psum[nc4][:], wh, mv, start=first, stop=False)
                                t.matmul(psum[nc4][:], wh, mvl, start=False, stop=False)
                                mm = t.matmul(psum[nc4][:], wll, mv, start=False,
                                              stop=last)
                            pw.inc(mm)

                    @block.scalar
                    def _(a, tc=tc, nbi=nbi):
                        a.wait_ge(pw.sem, pw.n)
                        a.wait_ge(ve.sem, ve.n)
                        a.wait_ge(pe.sem, pe.n)
                        for nc4 in range(4):
                            ae.inc(a.copy(stage[nc4][:], psum[nc4][:]))

                    if with_bias:
                        @block.vector
                        def _(v, nbi=nbi):
                            v.wait_ge(ae.sem, ae.n)
                            for nc4 in range(4):
                                idx = 4 * nbi + nc4
                                v.tensor_single_scalar(stage[nc4][:], stage[nc4][:],
                                                       sbias[:, idx:idx + 1], ALU.add)
                            ve.inc(v.drain())

                    for nc4 in range(4):
                        @block.tensor
                        def _(t, nc4=nc4):
                            t.wait_ge(ae.sem, ae.n)
                            t.wait_ge(ve.sem, ve.n)
                            for i in range(4):
                                tr = t.transpose(ptf[:, i * 128:(i + 1) * 128],
                                                 stage[nc4][:, i * 128:(i + 1) * 128],
                                                 identf[:])
                            pe.inc(tr)

                        @block.scalar
                        def _(a, nc4=nc4):
                            a.wait_ge(pe.sem, pe.n)
                            a.wait_ge(st.sem, st.n)
                            ae.inc(a.copy(stage2[:], ptf[:, :]))

                        @block.gpsimd
                        def _(g, tc=tc, nbi=nbi, nc4=nc4, dst_dram=dst_dram):
                            g.wait_ge(ae.sem, ae.n)
                            col0 = (4 * nbi + nc4) * 128
                            for i in range(4):
                                row0 = tc * 512 + i * 128
                                st.inc(g.dma_start(
                                    out=dst_dram[row0:row0 + 128, col0:col0 + 128],
                                    in_=stage2[:, i * 128:(i + 1) * 128]), 16)

        def ln_pass(a_dram_T, res_dram, dst_rows, also_split):
            for tt in range(NT):
                @block.sync
                def _(s, tt=tt):
                    s.wait_ge(ve.sem, ve.n)
                    s.wait_ge(st.sem, st.n)
                    ld.inc(s.dma_start(
                        out=xs[:, :], in_=a_dram_T[tt * 128:(tt + 1) * 128, :]), 16)
                    ld.inc(s.dma_start(out=xt2[:],
                                       in_=res_dram[tt * 128:(tt + 1) * 128, :]), 16)

                @block.vector
                def _(v, tt=tt):
                    v.wait_ge(ld.sem, ld.n)
                    v.wait_ge(ae.sem, ae.n)
                    v.tensor_add(xs[:], xs[:], xt2[:])
                    v.drain()
                    v.tensor_reduce(lm[:, :], xs[:], AX.X, ALU.add)
                    v.drain()
                    v.tensor_single_scalar(lm[:, :], lm[:, :], 1.0 / 4096.0, ALU.mult)
                    v.drain()
                    v.tensor_single_scalar(xs[:], xs[:], lm[:, 0:1], ALU.subtract)
                    v.drain()
                    for cchunk in range(4):
                        sl = slice(cchunk * 1024, (cchunk + 1) * 1024)
                        v.tensor_mul(scr["m0"][:, :], xs[:, sl], xs[:, sl])
                        v.drain()
                        v.tensor_reduce(scr["gm"][:, 0:1], scr["m0"][:, :], AX.X, ALU.add)
                        v.drain()
                        if cchunk == 0:
                            v.tensor_copy(lv[:, :], scr["gm"][:, 0:1])
                        else:
                            v.tensor_add(lv[:, :], lv[:, :], scr["gm"][:, 0:1])
                        v.drain()
                    ve.inc(v.drain())

                @block.scalar
                def _(a):
                    a.wait_ge(ve.sem, ve.n)
                    ae.inc(a.activation(lsd[:, :], lv[:, :], ACT.Sqrt,
                                        bias=ln_b[:, 0:1], scale=ln_s[:, 0:1]))

                @block.vector
                def _(v, tt=tt):
                    v.wait_ge(ae.sem, ae.n)
                    v.reciprocal(lrs[:, :], lsd[:, :])
                    v.drain()
                    v.tensor_single_scalar(xs[:], xs[:], lrs[:, 0:1], ALU.mult)
                    ve.inc(v.drain())

                @block.gpsimd
                def _(g, tt=tt):
                    g.wait_ge(ve.sem, ve.n)
                    st.inc(g.dma_start(out=dst_rows[tt * 128:(tt + 1) * 128, :],
                                       in_=xs[:]), 16)

                if also_split:
                    @block.vector
                    def _(v, tt=tt):
                        v.tensor_copy(xhi[:], xs[:])
                        v.drain()
                        v.tensor_sub(xlo[:], xs[:], xhi[:])
                        ve.inc(v.drain())

                    for kcg in range(8):
                        for part, buf, dstq in (("hi", xhi, xqhi), ("lo", xlo, xqlo)):
                            @block.tensor
                            def _(t, kcg=kcg, buf=buf):
                                t.wait_ge(ve.sem, ve.n)
                                t.wait_ge(ae.sem, ae.n)
                                for i in range(4):
                                    kc = kcg * 4 + i
                                    tr = t.transpose(pt[:, i * 128:(i + 1) * 128],
                                                     buf[:, kc * 128:(kc + 1) * 128],
                                                     ident[:])
                                pe.inc(tr)

                            @block.scalar
                            def _(a, tt=tt, kcg=kcg, dstq=dstq):
                                a.wait_ge(pe.sem, pe.n)
                                dst3 = dstq[:, :].rearrange("p (kc t) -> p kc t", t=TPC)
                                dst_sl = dst3[:, kcg * 4:(kcg + 1) * 4,
                                              tt * 128:(tt + 1) * 128]
                                ae.inc(a.copy(
                                    dst_sl,
                                    pt[:, :].rearrange("p (i q) -> p i q", q=128)))

        quant_split_transpose(o_in)
        projection(0, w_in["wohi"], w_in["wolo"], aT, with_bias=False)
        ln_pass(aT, x_in, x1r, also_split=True)
        projection(1, w_in["ffhi"], w_in["fflo"], fT, with_bias=True)
        ln_pass(fT, x1r, y_out, also_split=False)

        @block.gpsimd
        def _(g):
            g.wait_ge(st.sem, st.n)
    return nc



def kernel(L0_attn_hidden_states, L0_attn_Wq, L0_attn_Wk, L0_attn_Wv, L0_attn_Wo,
           L0_ffn_qweight, L0_ffn_scales, L0_ffn_scaled_zeros, L0_ffn_bias):
    f = np.float32
    cores = list(range(NCORE))
    x_full = np.ascontiguousarray(L0_attn_hidden_states.reshape(B * S, D), dtype=f)
    ident = np.eye(128, dtype=ml_dtypes.bfloat16)
    identf = np.eye(128, dtype=f)
    perm = host_ab_perm()
    Wq_p = np.ascontiguousarray(L0_attn_Wq[perm])
    Wk_p = np.ascontiguousarray(L0_attn_Wk[perm])
    fq = np.ascontiguousarray(L0_ffn_qweight.reshape(D, D), dtype=np.int32)
    fs = np.ascontiguousarray(L0_ffn_scales.reshape(D, 32), dtype=f)
    fz = np.ascontiguousarray(L0_ffn_scaled_zeros.reshape(D, 32), dtype=f)

    in0 = []
    for c in cores:
        sl = slice(c * 512, (c + 1) * 512)
        in0.append({"wq": Wq_p[sl], "wk": Wk_p[sl],
                    "wv": np.ascontiguousarray(L0_attn_Wv[sl]),
                    "wo": np.ascontiguousarray(L0_attn_Wo[sl]),
                    "fq": fq[sl], "fs": fs[sl], "fz": fz[sl], "ident": ident})
    r0 = run_bass_kernel_spmd(build_l0(), in0, core_ids=cores).results

    blocked = {}
    for m, ab in (("wq", True), ("wk", True), ("wv", False), ("wo", False), ("ff", False)):
        hi = np.concatenate([r0[c][m + "T_hi"] for c in cores], axis=2).reshape(D, D)
        lo = np.concatenate([r0[c][m + "T_lo"] for c in cores], axis=2).reshape(D, D)
        blocked[m] = host_block_w(hi, lo, ab)

    in1 = []
    for c in cores:
        rows = np.concatenate([np.arange(t * 128, t * 128 + 128) for t in core_tiles(c)])
        in1.append({"x": np.ascontiguousarray(x_full[rows]),
                    "pos": np.tile(host_positions(c), (128, 1)),
                    "ident": ident,
                    "wq_hi": blocked["wq"][0], "wq_lo": blocked["wq"][1],
                    "wk_hi": blocked["wk"][0], "wk_lo": blocked["wk"][1],
                    "wv_hi": blocked["wv"][0], "wv_lo": blocked["wv"][1]})
    r1 = run_bass_kernel_spmd(build_l1(), in1, core_ids=cores).results

    kT = {p: [] for p in ("hi", "lo")}
    vv = {p: [] for p in ("hi", "lo")}
    for b in range(B):
        for p in ("hi", "lo"):
            kb = np.empty((32, 128, S), dtype=ml_dtypes.bfloat16)
            vb = np.empty((32, 32, 128, 128), dtype=ml_dtypes.bfloat16)
            for pt in range(32):
                c = 4 * b + pt % 4
                j = pt // 4
                kb[:, :, pt * 128:(pt + 1) * 128] = r1[c]["kT_" + p][:, :, j * 128:(j + 1) * 128]
                vb[:, pt, :, :] = r1[c]["v_" + p][:, j, :, :]
            kT[p].append(kb)
            vv[p].append(vb)

    in2 = []
    for c in cores:
        b = c // 4
        in2.append({"qT_hi": r1[c]["qT_hi"], "qT_lo": r1[c]["qT_lo"],
                    "kT_hi": kT["hi"][b], "kT_lo": kT["lo"][b],
                    "v_hi": vv["hi"][b], "v_lo": vv["lo"][b],
                    "masks": host_masks(c), "ident": ident, "identf": identf})
    r2 = run_bass_kernel_spmd(build_l2(), in2, core_ids=cores).results

    bias_l3 = np.ascontiguousarray(np.asarray(L0_ffn_bias, dtype=f).reshape(32, 128).T)
    in3 = []
    for c in cores:
        rows = np.concatenate([np.arange(t * 128, t * 128 + 128) for t in core_tiles(c)])
        in3.append({"o": r2[c]["o"], "x": np.ascontiguousarray(x_full[rows]),
                    "ident": ident, "identf": identf, "bias": bias_l3,
                    "wo_hi": blocked["wo"][0], "wo_lo": blocked["wo"][1],
                    "ff_hi": blocked["ff"][0], "ff_lo": blocked["ff"][1]})
    r3 = run_bass_kernel_spmd(build_l3(), in3, core_ids=cores).results

    out = np.empty((B * S, D), dtype=f)
    for c in cores:
        rows = np.concatenate([np.arange(t * 128, t * 128 + 128) for t in core_tiles(c)])
        out[rows] = r3[c]["y"]
    return out.reshape(B, S, D)


# version 03
# speedup vs baseline: 1.0039x; 1.0039x over previous
import math
from contextlib import ExitStack

import numpy as np
import ml_dtypes

import concourse.bass as bass
import concourse.mybir as mybir
from concourse.bass_utils import run_bass_kernel_spmd

F32 = mybir.dt.float32
BF16 = mybir.dt.bfloat16
I32 = mybir.dt.int32
AX = mybir.AxisListType
ALU = mybir.AluOpType
ACT = mybir.ActivationFunctionType

D = 4096
B = 2
S = 4096
NCORE = 8
TPC = 1024
NT = TPC // 128
NKC = D // 128
HEADS = 32
HDIM = 128
SEGS = [(0, 2048, 7.0, 4), (2048, 3072, 31.0, 6), (3072, 4096, 127.0, 8)]
SOFTMAX_SHIFT = 40.0
LN_EPS = 1e-5
NEG_BIG = -3.0e38


class Sem:
    def __init__(self, sem):
        self.sem = sem
        self.n = 0

    def inc(self, instr, by=1):
        instr.then_inc(self.sem, by)
        self.n += by
        return self.n


def core_tiles(c):
    b = c // 4
    return [b * 32 + (c % 4) + 4 * j for j in range(NT)]


def seq_tiles(c):
    return [(c % 4) + 4 * j for j in range(NT)]



def emit_quant(v, src, dst, scr, P=128, segs=None):
    dr = v.drain
    src3 = src[:P, :].rearrange("p (g k) -> p g k", k=32)
    dst3 = dst[:P, :].rearrange("p (g k) -> p g k", k=32)
    chunks = []
    for (lo, hi, maxq, bits) in (segs if segs is not None else SEGS):
        for c in range(lo, hi, 1024):
            chunks.append((c, min(c + 1024, hi), maxq, bits))
    for (lo, hi, maxq, bits) in chunks:
        W = hi - lo
        G = W // 32
        assert W == 1024 and G == 32
        p2 = float(1 << (bits - 1))
        g3 = src3[:, lo // 32:hi // 32, :]
        gmax, s0, t1, eps, u = (scr[k][:P, :G] for k in ("gm", "s0", "t1", "eps", "u"))
        m0, m, ct = (scr[k][:P, :W] for k in ("m0", "m", "ct"))
        rmq = 1.0 / maxq
        v.tensor_reduce(gmax, g3, AX.X, ALU.max, apply_absolute_value=True)
        dr()
        v.tensor_single_scalar(s0, gmax, rmq, ALU.mult)
        dr()
        for _ in range(2):
            v.scalar_tensor_tensor(t1, s0, p2, gmax, ALU.mult, ALU.subtract)
            dr()
            v.tensor_sub(eps, t1, s0)
            dr()
            v.tensor_single_scalar(u, eps, rmq, ALU.mult)
            dr()
            v.tensor_sub(s0, s0, u)
            dr()
        v.tensor_single_scalar(t1, gmax, 0.0, ALU.is_equal)
        dr()
        v.tensor_add(s0, s0, t1)
        dr()
        v.reciprocal(u, s0)
        dr()
        rb = u.unsqueeze(2).broadcast_to([P, G, 32])
        v.tensor_tensor(m0.rearrange("p (g k) -> p g k", k=32), g3, rb, ALU.mult)
        dr()
        v.tensor_single_scalar(m0, m0, 0.5 + p2, ALU.add)
        dr()
        for j in range(bits - 1, -1, -1):
            tj = float(1 << j)
            src_m = m0 if j == bits - 1 else m
            v.tensor_scalar(ct, src_m, tj, tj, ALU.is_ge, ALU.mult)
            dr()
            v.tensor_sub(m, src_m, ct)
            dr()
        v.scalar_tensor_tensor(ct, m0, -p2, m, ALU.add, ALU.subtract)
        dr()
        sb = s0.unsqueeze(2).broadcast_to([P, G, 32])
        v.tensor_tensor(dst3[:, lo // 32:hi // 32, :],
                        ct.rearrange("p (g k) -> p g k", k=32), sb, ALU.mult)
        dr()


QUANT_SCRATCH = [("gm", [128, 32]), ("s0", [128, 32]), ("t1", [128, 32]),
                 ("eps", [128, 32]), ("u", [128, 32]),
                 ("m0", [128, 1024]), ("m", [128, 1024]), ("ct", [128, 1024])]


def alloc_quant_scratch(nc, prefix=""):
    return {k: nc.alloc_sbuf_tensor(prefix + "qs_" + k, sh, F32).ap()
            for k, sh in QUANT_SCRATCH}



def build_l0():
    nc = bass.Bass("TRN2", target_bir_lowering=False)
    ins = {}
    outs = {}
    for m in ("wq", "wk", "wv", "wo"):
        ins[m] = nc.dram_tensor(m, [512, 4096], F32, kind="ExternalInput").ap()
    ins["fq"] = nc.dram_tensor("fq", [512, 4096], I32, kind="ExternalInput").ap()
    ins["fs"] = nc.dram_tensor("fs", [512, 32], F32, kind="ExternalInput").ap()
    ins["fz"] = nc.dram_tensor("fz", [512, 32], F32, kind="ExternalInput").ap()
    ins["ident"] = nc.dram_tensor("ident", [128, 128], BF16, kind="ExternalInput").ap()
    for m in ("wq", "wk", "wv", "wo", "ff"):
        for p in ("hi", "lo"):
            outs[m + p] = nc.dram_tensor(m + "T_" + p, [32, 128, 512], BF16,
                                         kind="ExternalOutput").ap()

    stage = nc.alloc_sbuf_tensor("stage", [128, 4096], F32).ap()
    stage_i = nc.alloc_sbuf_tensor("stage_i", [128, 4096], I32).ap()
    sfs = nc.alloc_sbuf_tensor("sfs", [128, 32], F32).ap()
    sfz = nc.alloc_sbuf_tensor("sfz", [128, 32], F32).ap()
    dq = nc.alloc_sbuf_tensor("dq", [128, 4096], F32).ap()
    hi = nc.alloc_sbuf_tensor("hi", [128, 4096], BF16).ap()
    lo = nc.alloc_sbuf_tensor("lo", [128, 4096], BF16).ap()
    ident = nc.alloc_sbuf_tensor("identt", [128, 128], BF16).ap()
    ev_hi = nc.alloc_sbuf_tensor("ev_hi", [128, 128], BF16).ap()
    ev_lo = nc.alloc_sbuf_tensor("ev_lo", [128, 128], BF16).ap()
    scr = alloc_quant_scratch(nc)

    with ExitStack() as ctx:
        pt_hi = ctx.enter_context(nc.psum_tensor("pt_hi", [128, 128], BF16))
        pt_lo = ctx.enter_context(nc.psum_tensor("pt_lo", [128, 128], BF16))
        ld = Sem(ctx.enter_context(nc.semaphore("ld")))
        ve = Sem(ctx.enter_context(nc.semaphore("ve")))
        pe = Sem(ctx.enter_context(nc.semaphore("pe")))
        ae = Sem(ctx.enter_context(nc.semaphore("ae")))
        st = Sem(ctx.enter_context(nc.semaphore("st")))
        block = ctx.enter_context(nc.Block())

        MATS = [("wq", True), ("wk", True), ("wv", True), ("wo", True), ("ff", False)]

        @block.sync
        def _(sync):
            sync.dma_start(out=ident[:], in_=ins["ident"][:, :]).then_inc(ld.sem, 16)
            ld.n += 16
            sync.dma_start(out=sfs[:], in_=ins["fs"][:128, :]).then_inc(ld.sem, 16)
            ld.n += 16
            sync.dma_start(out=sfz[:], in_=ins["fz"][:128, :]).then_inc(ld.sem, 16)
            ld.n += 16
            for mi, (m, is_attn) in enumerate(MATS):
                for rt in range(4):
                    need = mi * 4 + rt - 1
                    if need >= 0:
                        sync.wait_ge(ve.sem, need + 1)
                    if is_attn:
                        sync.dma_start(out=stage[:], in_=ins[m][rt * 128:(rt + 1) * 128, :]
                                       ).then_inc(ld.sem, 16)
                    else:
                        sync.dma_start(out=stage_i[:], in_=ins["fq"][rt * 128:(rt + 1) * 128, :]
                                       ).then_inc(ld.sem, 16)
                        sync.dma_start(out=sfs[:], in_=ins["fs"][rt * 128:(rt + 1) * 128, :]
                                       ).then_inc(ld.sem, 16)
                        sync.dma_start(out=sfz[:], in_=ins["fz"][rt * 128:(rt + 1) * 128, :]
                                       ).then_inc(ld.sem, 16)

        lcnt = 48
        lcnt_at = {}
        for mi, (m, is_attn) in enumerate(MATS):
            for rt in range(4):
                lcnt += 16 if is_attn else 48
                lcnt_at[(mi, rt)] = lcnt

        @block.vector
        def _(vector):
            for mi, (m, is_attn) in enumerate(MATS):
                for rt in range(4):
                    vector.wait_ge(ld.sem, lcnt_at[(mi, rt)])
                    t_idx = mi * 4 + rt
                    if t_idx >= 1:
                        vector.wait_ge(pe.sem, 32 * t_idx)
                    if is_attn:
                        emit_quant(vector, stage, dq, scr)
                    else:
                        q3 = stage_i[:, :].rearrange("p (g k) -> p g k", k=128)
                        d3 = dq[:, :].rearrange("p (g k) -> p g k", k=128)
                        sb = sfs[:, :].unsqueeze(2).broadcast_to([128, 32, 128])
                        zb = sfz[:, :].unsqueeze(2).broadcast_to([128, 32, 128])
                        vector.tensor_tensor(d3, q3, sb, ALU.mult)
                        vector.drain()
                        vector.tensor_tensor(d3, d3, zb, ALU.add)
                        vector.drain()
                    vector.tensor_copy(hi[:], dq[:])
                    vector.drain()
                    vector.tensor_sub(lo[:], dq[:], hi[:])
                    vector.drain()
                    ve.inc(vector.engine_nop())

        @block.tensor
        def _(tensor):
            for mi, (m, is_attn) in enumerate(MATS):
                for rt in range(4):
                    tensor.wait_ge(ve.sem, mi * 4 + rt + 1)
                    for kc in range(32):
                        done_pairs = (mi * 4 + rt) * 32 + kc
                        if done_pairs >= 1:
                            tensor.wait_ge(ae.sem, done_pairs)
                        tensor.transpose(pt_hi[:], hi[:, kc * 128:(kc + 1) * 128], ident[:])
                        pe.inc(tensor.transpose(pt_lo[:], lo[:, kc * 128:(kc + 1) * 128], ident[:]))

        @block.scalar
        def _(scalar):
            for mi, (m, is_attn) in enumerate(MATS):
                for rt in range(4):
                    for kc in range(32):
                        n_pair = (mi * 4 + rt) * 32 + kc + 1
                        scalar.wait_ge(pe.sem, n_pair)
                        if n_pair > 1:
                            scalar.wait_ge(st.sem, 32 * (n_pair - 1))
                        scalar.copy(ev_hi[:], pt_hi[:])
                        ae_n = scalar.copy(ev_lo[:], pt_lo[:])
                        ae.inc(ae_n)

        @block.gpsimd
        def _(gpsimd):
            for mi, (m, is_attn) in enumerate(MATS):
                for rt in range(4):
                    for kc in range(32):
                        n_pair = (mi * 4 + rt) * 32 + kc + 1
                        gpsimd.wait_ge(ae.sem, n_pair)
                        gpsimd.dma_start(out=outs[m + "hi"][kc, :, rt * 128:(rt + 1) * 128],
                                         in_=ev_hi[:]).then_inc(st.sem, 16)
                        gpsimd.dma_start(out=outs[m + "lo"][kc, :, rt * 128:(rt + 1) * 128],
                                         in_=ev_lo[:]).then_inc(st.sem, 16)
                        st.n += 32
            gpsimd.wait_ge(st.sem, st.n)
    return nc



def emit_floorhalf(v, src, dst, w1, bits, base):
    dr = v.drain
    v.tensor_single_scalar(dst, src, 0.5, ALU.add)
    dr()
    for j in range(bits - 1, -1, -1):
        tj = float(1 << j)
        sm = dst if j == bits - 1 else w1
        v.tensor_scalar(base, sm, tj, tj, ALU.is_ge, ALU.mult)
        dr()
        v.tensor_sub(w1, sm, base)
        dr()
    v.tensor_sub(dst, dst, w1)
    dr()



def build_l1(do_b=True, nproj=3, nnblk=8, c_stop='all', do_a=True):
    nc = bass.Bass("TRN2", target_bir_lowering=False)
    x_in = nc.dram_tensor("x", [TPC, D], F32, kind="ExternalInput").ap()
    pos_in = nc.dram_tensor("pos", [128, TPC], F32, kind="ExternalInput").ap()
    ident_in = nc.dram_tensor("ident", [128, 128], BF16, kind="ExternalInput").ap()
    w_in = {}
    for m in ("q", "k", "v"):
        for p in ("hi", "lo"):
            w_in[m + p] = nc.dram_tensor("w" + m + "_" + p, [8, 32, 128, 512], BF16,
                                         kind="ExternalInput").ap()
    o_out = {}
    for m in ("q", "k"):
        for p in ("hi", "lo"):
            o_out[m + p] = nc.dram_tensor(m + "T_" + p, [32, 128, TPC], BF16,
                                          kind="ExternalOutput").ap()
    for p in ("hi", "lo"):
        o_out["v" + p] = nc.dram_tensor("v_" + p, [32, NT, 128, 128], BF16,
                                        kind="ExternalOutput").ap()

    xs = nc.alloc_sbuf_tensor("xs", [128, D], F32).ap()
    xhi = nc.alloc_sbuf_tensor("xhi", [128, D], BF16).ap()
    xlo = nc.alloc_sbuf_tensor("xlo", [128, D], BF16).ap()
    xqhi = nc.alloc_sbuf_tensor("xqhi", [128, NKC * TPC], BF16).ap()
    xqlo = nc.alloc_sbuf_tensor("xqlo", [128, NKC * TPC], BF16).ap()
    ident = nc.alloc_sbuf_tensor("identt", [128, 128], BF16).ap()
    scr = alloc_quant_scratch(nc)
    pos_rep = nc.alloc_sbuf_tensor("pos_rep", [128, TPC], F32).ap()
    ones64 = nc.alloc_sbuf_tensor("ones64", [1, 64], F32).ap()
    cos_tab = nc.alloc_sbuf_tensor("cos_tab", [128, TPC], F32).ap()
    sin_tab = nc.alloc_sbuf_tensor("sin_tab", [128, TPC], F32).ap()
    big2 = [nc.alloc_sbuf_tensor(f"big{i}", [128, TPC], F32).ap() for i in range(2)]
    c_sc = nc.alloc_sbuf_tensor("c_sc", [1, 1], F32).ap()
    c_ln = nc.alloc_sbuf_tensor("c_ln", [1, 1], F32).ap()
    c_rep = nc.alloc_sbuf_tensor("c_rep", [64, 1], F32).ap()
    iot = nc.alloc_sbuf_tensor("iot", [64, 1], F32).ap()
    iexp = nc.alloc_sbuf_tensor("iexp", [64, 1], F32).ap()
    inv_t = nc.alloc_sbuf_tensor("inv_t", [128, 1], F32).ap()
    wbuf = {(p, b): nc.alloc_sbuf_tensor(f"wbuf_{p}{b}", [128, 512], BF16).ap()
            for p in ("hi", "lo") for b in range(2)}
    stage = [big2[0][:, 0:512], big2[0][:, 512:1024],
             big2[1][:, 0:512], big2[1][:, 512:1024]]
    ro = [scr["m0"][:, 0:512], scr["m0"][:, 512:1024],
          scr["m"][:, 0:512], scr["m"][:, 512:1024]]
    rt1 = nc.alloc_sbuf_tensor("rt1", [128, 512], F32).ap()
    rt2 = nc.alloc_sbuf_tensor("rt2", [128, 512], F32).ap()
    ang, tw3 = big2[0], big2[1]
    tw1, tw2 = scr["m0"][:, :], scr["m"][:, :]
    shi = [xhi[:, i * 512:(i + 1) * 512] for i in range(4)]
    slo = [xlo[:, i * 512:(i + 1) * 512] for i in range(4)]
    vstg = xhi[:, 2048:2560]
    vstg2 = xhi[:, 2560:3072]

    TWO_PI = 2.0 * math.pi
    C1 = float(np.float32(TWO_PI))
    C2 = float(np.float64(TWO_PI) - np.float64(np.float32(TWO_PI)))
    PI_CLAMP = 3.1415925

    with ExitStack() as ctx:
        psum = [ctx.enter_context(nc.psum_tensor(f"pj{i}", [128, 512], F32))
                for i in range(4)]
        pt = ctx.enter_context(nc.psum_tensor("pt", [128, 512], BF16))
        ptf = ctx.enter_context(nc.psum_tensor("ptf", [128, 512], F32))
        ld = Sem(ctx.enter_context(nc.semaphore("ld")))
        wl = Sem(ctx.enter_context(nc.semaphore("wl")))
        ve = Sem(ctx.enter_context(nc.semaphore("ve")))
        pe = Sem(ctx.enter_context(nc.semaphore("pe")))
        pw = Sem(ctx.enter_context(nc.semaphore("pw")))
        ae = Sem(ctx.enter_context(nc.semaphore("ae")))
        ge = Sem(ctx.enter_context(nc.semaphore("ge")))
        st = Sem(ctx.enter_context(nc.semaphore("st")))
        block = ctx.enter_context(nc.Block())

        @block.sync
        def _(s):
            ld.inc(s.dma_start(out=ident[:], in_=ident_in[:, :]), 16)
            ld.inc(s.dma_start(out=pos_rep[:], in_=pos_in[:, :]), 16)

        if not do_a:
            pass

        if do_a:
          @block.gpsimd
          def _(g):
            g.wait_ge(ld.sem, ld.n)
            g.memset(c_sc[:, :], 10000.0)
            g.memset(ones64[:, :], 1.0)
            g.iota(iot[:, :], [[0, 1]], base=0, channel_multiplier=1,
                   allow_small_or_imprecise_dtypes=True)
            ge.inc(g.engine_nop())

          @block.scalar
          def _(a):
              a.wait_ge(ge.sem, ge.n)
              ae.inc(a.activation(c_ln[:, :], c_sc[:, :], ACT.Ln))

          @block.vector
          def _(v):
              v.wait_ge(ae.sem, ae.n)
              v.tensor_single_scalar(c_ln[:, :], c_ln[:, :], -1.0 / 64.0, ALU.mult)
              ve.inc(v.drain())

          @block.tensor
          def _(t):
              t.wait_ge(ve.sem, ve.n)
              pe.inc(t.matmul(psum[0][0:64, 0:1], ones64[0:1, :], c_ln[0:1, :],
                              start=True, stop=True))

          @block.scalar
          def _(a):
              a.wait_ge(pe.sem, pe.n)
              ae.inc(a.copy(c_rep[:, :], psum[0][0:64, 0:1]))

          @block.vector
          def _(v):
              v.wait_ge(ae.sem, ae.n)
              v.tensor_tensor(iexp[:, :], iot[:, :], c_rep[:, :], ALU.mult)
              ve.inc(v.drain())

          @block.scalar
          def _(a):
              a.wait_ge(ve.sem, ve.n)
              ae.inc(a.activation(inv_t[0:64, :], iexp[:, :], ACT.Exp))

          @block.sync
          def _(s):
              s.wait_ge(ae.sem, ae.n)
              ld.inc(s.dma_start(out=inv_t[64:128, :], in_=inv_t[0:64, :]), 16)

          for trig in ("sin", "cos"):
              @block.vector
              def _(v, trig=trig):
                  v.wait_ge(ld.sem, ld.n)
                  v.wait_ge(ae.sem, ae.n)
                  if trig == "sin":
                      v.tensor_single_scalar(ang[:, :], pos_rep[:, :], inv_t[:, 0:1],
                                             ALU.mult)
                  else:
                      v.tensor_scalar(ang[:, :], pos_rep[:, :], inv_t[:, 0:1],
                                      float(np.float32(math.pi / 2)), ALU.mult, ALU.add)
                  v.drain()
                  v.tensor_single_scalar(tw3[:, :], ang[:, :], 1.0 / TWO_PI, ALU.mult)
                  v.drain()
                  emit_floorhalf(v, tw3, tw1, tw2, 11, scr["ct"][:, :TPC])
                  v.scalar_tensor_tensor(tw2[:, :], tw1[:, :], -C1, ang[:, :],
                                         ALU.mult, ALU.add)
                  v.drain()
                  v.scalar_tensor_tensor(tw3[:, :], tw1[:, :], -C2, tw2[:, :],
                                         ALU.mult, ALU.add)
                  v.drain()
                  v.tensor_scalar(tw3[:, :], tw3[:, :], -PI_CLAMP, PI_CLAMP,
                                  ALU.max, ALU.min)
                  ve.inc(v.drain())

              @block.scalar
              def _(a, trig=trig):
                  a.wait_ge(ve.sem, ve.n)
                  dst = sin_tab if trig == "sin" else cos_tab
                  ae.inc(a.activation(dst[:, :], tw3[:, :], ACT.Sin))


        for tt in range(NT if do_b else 0):
            @block.sync
            def _(s, tt=tt):
                s.wait_ge(ve.sem, ve.n)
                ld.inc(s.dma_start(out=xs[:], in_=x_in[tt * 128:(tt + 1) * 128, :]), 16)

            @block.vector
            def _(v, tt=tt):
                v.wait_ge(ld.sem, ld.n)
                v.wait_ge(ae.sem, ae.n)
                emit_quant(v, xs, xs, scr)
                v.tensor_copy(xhi[:], xs[:])
                v.drain()
                v.tensor_sub(xlo[:], xs[:], xhi[:])
                ve.inc(v.drain())

            for kcg in range(8):
                for part, buf, dstq in (("hi", xhi, xqhi), ("lo", xlo, xqlo)):
                    @block.tensor
                    def _(t, kcg=kcg, buf=buf):
                        t.wait_ge(ve.sem, ve.n)
                        t.wait_ge(ae.sem, ae.n)
                        for i in range(4):
                            kc = kcg * 4 + i
                            tr = t.transpose(pt[:, i * 128:(i + 1) * 128],
                                             buf[:, kc * 128:(kc + 1) * 128], ident[:])
                        pe.inc(tr)

                    @block.scalar
                    def _(a, tt=tt, kcg=kcg, dstq=dstq):
                        a.wait_ge(pe.sem, pe.n)
                        dst3 = dstq[:, :].rearrange("p (kc t) -> p kc t", t=TPC)
                        dst_sl = dst3[:, kcg * 4:(kcg + 1) * 4, tt * 128:(tt + 1) * 128]
                        ae.inc(a.copy(dst_sl,
                                      pt[:, :].rearrange("p (i q) -> p i q", q=128)))

        for pi, proj in enumerate(("q", "k", "v")[:nproj]):
            whi_d, wlo_d = w_in[proj + "hi"], w_in[proj + "lo"]
            for tc in range(2):
                for nbi in range(nnblk):
                    @block.sync
                    def _(s, pi=pi, tc=tc, nbi=nbi, whi_d=whi_d, wlo_d=wlo_d):
                        for kc in range(32):
                            slab = ((pi * 2 + tc) * 8 + nbi) * 32 + kc
                            if slab >= 2:
                                s.wait_ge(pw.sem, slab - 1)
                            b = slab % 2
                            wl.inc(s.dma_start(out=wbuf[("hi", b)][:],
                                               in_=whi_d[nbi, kc, :, :]), 16)
                            wl.inc(s.dma_start(out=wbuf[("lo", b)][:],
                                               in_=wlo_d[nbi, kc, :, :]), 16)

                    @block.tensor
                    def _(t, pi=pi, tc=tc, nbi=nbi):
                        t.wait_ge(ae.sem, ae.n)
                        for kc in range(32):
                            slab = ((pi * 2 + tc) * 8 + nbi) * 32 + kc
                            t.wait_ge(wl.sem, 64 if slab <= 1 else 32 * (slab + 1))
                            b = slab % 2
                            first = kc == 0
                            last = kc == 31
                            for nc4 in range(4):
                                mv = xqhi[:, kc * TPC + tc * 512: kc * TPC + tc * 512 + 512]
                                mvl = xqlo[:, kc * TPC + tc * 512: kc * TPC + tc * 512 + 512]
                                wh = wbuf[("hi", b)][:, nc4 * 128:(nc4 + 1) * 128]
                                wll = wbuf[("lo", b)][:, nc4 * 128:(nc4 + 1) * 128]
                                t.matmul(psum[nc4][:], wh, mv, start=first, stop=False)
                                t.matmul(psum[nc4][:], wh, mvl, start=False, stop=False)
                                mm = t.matmul(psum[nc4][:], wll, mv, start=False, stop=last)
                            pw.inc(mm)

                    if c_stop == 'mm':
                        continue

                    @block.scalar
                    def _(a):
                        a.wait_ge(pw.sem, pw.n)
                        a.wait_ge(ve.sem, ve.n)
                        for nc4 in range(4):
                            ae.inc(a.copy(stage[nc4][:], psum[nc4][:]))
                    if c_stop == 'evac':
                        continue

                    if proj in ("q", "k"):
                        @block.vector
                        def _(v, tc=tc):
                            v.wait_ge(ae.sem, ae.n)
                            v.wait_ge(st.sem, st.n)
                            cs = cos_tab[:, tc * 512:(tc + 1) * 512]
                            sn = sin_tab[:, tc * 512:(tc + 1) * 512]
                            for pr in range(2):
                                A, Bc = stage[pr], stage[2 + pr]
                                oA, oB = ro[pr], ro[2 + pr]
                                v.tensor_tensor(rt1[:], A[:], cs, ALU.mult)
                                v.tensor_tensor(rt2[:], Bc[:], sn, ALU.mult)
                                v.drain()
                                v.tensor_sub(oA[:], rt1[:], rt2[:])
                                v.tensor_tensor(rt1[:], Bc[:], cs, ALU.mult)
                                v.tensor_tensor(rt2[:], A[:], sn, ALU.mult)
                                v.drain()
                                v.tensor_add(oB[:], rt1[:], rt2[:])
                                v.drain()
                            for i in range(4):
                                v.tensor_copy(shi[i][:], ro[i][:])
                            v.drain()
                            for i in range(4):
                                v.tensor_sub(slo[i][:], ro[i][:], shi[i][:])
                            ve.inc(v.drain())

                        @block.gpsimd
                        def _(g, proj=proj, tc=tc, nbi=nbi):
                            g.wait_ge(ve.sem, ve.n)
                            dh, dl = o_out[proj + "hi"], o_out[proj + "lo"]
                            tsl = slice(tc * 512, tc * 512 + 512)
                            for pr in range(2):
                                for hh in range(2):
                                    h = 4 * nbi + 2 * pr + hh
                                    sl = slice(hh * 64, hh * 64 + 64)
                                    st.inc(g.dma_start(out=dh[h, 0:64, tsl],
                                                       in_=shi[pr][sl, :]), 16)
                                    st.inc(g.dma_start(out=dh[h, 64:128, tsl],
                                                       in_=shi[2 + pr][sl, :]), 16)
                                    st.inc(g.dma_start(out=dl[h, 0:64, tsl],
                                                       in_=slo[pr][sl, :]), 16)
                                    st.inc(g.dma_start(out=dl[h, 64:128, tsl],
                                                       in_=slo[2 + pr][sl, :]), 16)
                    else:
                        for nc4 in range(4):
                            @block.vector
                            def _(v, nc4=nc4):
                                v.wait_ge(ae.sem, ae.n)
                                if nc4 == 0:
                                    v.wait_ge(st.sem, st.n)
                                v.wait_ge(pe.sem, pe.n)
                                v.tensor_copy(shi[0][:], stage[nc4][:])
                                v.drain()
                                v.tensor_sub(slo[0][:], stage[nc4][:], shi[0][:])
                                ve.inc(v.drain())

                            for part, sbuf_, vdst in (("hi", shi, vstg), ("lo", slo, vstg2)):
                                @block.tensor
                                def _(t, sbuf_=sbuf_):
                                    t.wait_ge(ve.sem, ve.n)
                                    t.wait_ge(ae.sem, ae.n)
                                    for i in range(4):
                                        tr = t.transpose(pt[:, i * 128:(i + 1) * 128],
                                                         sbuf_[0][:, i * 128:(i + 1) * 128],
                                                         ident[:])
                                    pe.inc(tr)

                                @block.scalar
                                def _(a, vdst=vdst):
                                    a.wait_ge(pe.sem, pe.n)
                                    a.wait_ge(st.sem, st.n)
                                    ae.inc(a.copy(vdst[:], pt[:]))

                            @block.gpsimd
                            def _(g, tc=tc, nbi=nbi, nc4=nc4):
                                g.wait_ge(ae.sem, ae.n)
                                h = 4 * nbi + nc4
                                for i in range(4):
                                    ttile = tc * 4 + i
                                    st.inc(g.dma_start(
                                        out=o_out["vhi"][h, ttile, :, :],
                                        in_=vstg[:, i * 128:(i + 1) * 128]), 16)
                                    st.inc(g.dma_start(
                                        out=o_out["vlo"][h, ttile, :, :],
                                        in_=vstg2[:, i * 128:(i + 1) * 128]), 16)

        @block.gpsimd
        def _(g):
            g.wait_ge(st.sem, st.n)
    return nc



def host_ab_perm():
    idx = np.arange(D).reshape(HEADS, 2, 64)
    return np.concatenate([idx[:, 0, :].ravel(), idx[:, 1, :].ravel()])


def host_block_w(wT_hi, wT_lo, ab):
    out = []
    for t in (wT_hi, wT_lo):
        a = t.reshape(32, 128, 32, 128)
        blocks = []
        for j in range(8):
            chunks = [2 * j, 2 * j + 1, 16 + 2 * j, 16 + 2 * j + 1] if ab else \
                     [4 * j, 4 * j + 1, 4 * j + 2, 4 * j + 3]
            b = a[:, :, chunks, :]
            blocks.append(b.reshape(32, 128, 512))
        out.append(np.ascontiguousarray(np.stack(blocks, axis=0)))
    return out[0], out[1]


def host_positions(c):
    ps = []
    for p in seq_tiles(c):
        ps.extend(range(p * 128, p * 128 + 128))
    return np.asarray(ps, dtype=np.float32).reshape(1, TPC)



def l2_jmin(kc):
    return max(0, -(-(kc - 3) // 4))


def build_l2():
    nc = bass.Bass("TRN2", target_bir_lowering=False)
    ins = {}
    for m in ("q", "k"):
        for p in ("hi", "lo"):
            ins[m + p] = nc.dram_tensor(m + "T_" + p, [32, 128, S if m == "k" else TPC],
                                        BF16, kind="ExternalInput").ap()
    for p in ("hi", "lo"):
        ins["v" + p] = nc.dram_tensor("v_" + p, [32, 32, 128, 128], BF16,
                                      kind="ExternalInput").ap()
    masks_in = nc.dram_tensor("masks", [32, 128, 128], F32, kind="ExternalInput").ap()
    ident_in = nc.dram_tensor("ident", [128, 128], BF16, kind="ExternalInput").ap()
    identf_in = nc.dram_tensor("identf", [128, 128], F32, kind="ExternalInput").ap()
    o_out = nc.dram_tensor("o", [TPC, D], F32, kind="ExternalOutput").ap()

    sq = {p: nc.alloc_sbuf_tensor(f"sq_{p}", [128, TPC], BF16).ap() for p in ("hi", "lo")}
    sk = {p: nc.alloc_sbuf_tensor(f"sk_{p}", [128, S], BF16).ap() for p in ("hi", "lo")}
    sv = {p: nc.alloc_sbuf_tensor(f"sv_{p}", [128, 32 * 128], BF16).ap() for p in ("hi", "lo")}
    smask = nc.alloc_sbuf_tensor("smask", [128, 32 * 128], F32).ap()
    ident = nc.alloc_sbuf_tensor("identt", [128, 128], BF16).ap()
    identf = nc.alloc_sbuf_tensor("identtf", [128, 128], F32).ap()
    onesk = nc.alloc_sbuf_tensor("onesk", [128, 1], BF16).ap()
    zerok = nc.alloc_sbuf_tensor("zerok", [128, 1], BF16).ap()
    pf = nc.alloc_sbuf_tensor("pf", [128, TPC], F32).ap()
    phi = nc.alloc_sbuf_tensor("phi", [128, TPC], BF16).ap()
    plo = nc.alloc_sbuf_tensor("plo", [128, TPC], BF16).ap()
    osb = nc.alloc_sbuf_tensor("osb", [128, TPC], F32).ap()
    zr = nc.alloc_sbuf_tensor("zr", [1, TPC], F32).ap()
    zrs = nc.alloc_sbuf_tensor("zrs", [128, NT], F32).ap()
    otm = nc.alloc_sbuf_tensor("otm", [128, NT * 128], F32).ap()
    expb = nc.alloc_sbuf_tensor("expb", [128, 1], F32).ap()
    exps = nc.alloc_sbuf_tensor("exps", [128, 1], F32).ap()

    SCALE = 1.0 / math.sqrt(128.0)

    with ExitStack() as ctx:
        pS = [ctx.enter_context(nc.psum_tensor(f"pS{i}", [128, 512], F32)) for i in range(2)]
        pO = [ctx.enter_context(nc.psum_tensor(f"pO{i}", [128, 512], F32)) for i in range(2)]
        pZ = [ctx.enter_context(nc.psum_tensor(f"pZ{i}", [1, 512], F32)) for i in range(2)]
        pzt = ctx.enter_context(nc.psum_tensor("pzt", [128, NT], F32))
        ptm = ctx.enter_context(nc.psum_tensor("ptm", [128, 512], F32))
        ld = Sem(ctx.enter_context(nc.semaphore("ld")))
        ve = Sem(ctx.enter_context(nc.semaphore("ve")))
        pe = Sem(ctx.enter_context(nc.semaphore("pe")))
        ae = Sem(ctx.enter_context(nc.semaphore("ae")))
        ge = Sem(ctx.enter_context(nc.semaphore("ge")))
        st = Sem(ctx.enter_context(nc.semaphore("st")))
        block = ctx.enter_context(nc.Block())

        @block.sync
        def _(s):
            ld.inc(s.dma_start(out=ident[:], in_=ident_in[:, :]), 16)
            ld.inc(s.dma_start(out=identf[:], in_=identf_in[:, :]), 16)
            ld.inc(s.dma_start(
                out=smask[:, :].rearrange("p (kc q) -> p kc q", q=128),
                in_=masks_in[:, :, :].transpose([1, 0, 2])), 16)

        @block.gpsimd
        def _(g):
            g.memset(onesk[:, :], 1.0)
            g.memset(zerok[:, :], 0.0)
            g.memset(expb[:, :], -SOFTMAX_SHIFT)
            g.memset(exps[:, :], SCALE)
            ge.inc(g.engine_nop())

        for h in range(HEADS):
            @block.sync
            def _(s, h=h):
                s.wait_ge(ve.sem, ve.n)
                s.wait_ge(st.sem, st.n)
                for p in ("hi", "lo"):
                    ld.inc(s.dma_start(out=sq[p][:], in_=ins["q" + p][h, :, :]), 16)
                    ld.inc(s.dma_start(out=sk[p][:], in_=ins["k" + p][h, :, :]), 16)
                    ld.inc(s.dma_start(
                        out=sv[p][:, :].rearrange("t (kc d) -> t kc d", d=128),
                        in_=ins["v" + p][h, :, :, :].transpose([1, 0, 2])), 16)

            for kc in range(32):
                q0 = l2_jmin(kc) * 128
                chunks = []
                for c in range(2):
                    lo_c = max(q0, c * 512)
                    hi_c = (c + 1) * 512
                    if lo_c < hi_c:
                        chunks.append((c, lo_c, hi_c))

                @block.tensor
                def _(t, h=h, kc=kc, q0=q0, chunks=chunks):
                    t.wait_ge(ld.sem, ld.n)
                    if kc > 0:
                        t.wait_ge(ae.sem, ae.n)
                    kh = sk["hi"][:, kc * 128:(kc + 1) * 128]
                    kl = sk["lo"][:, kc * 128:(kc + 1) * 128]
                    for (c, lo_c, hi_c) in chunks:
                        r = slice(lo_c - c * 512, hi_c - c * 512)
                        qh = sq["hi"][:, lo_c:hi_c]
                        ql = sq["lo"][:, lo_c:hi_c]
                        t.matmul(pS[c][:, r], kh, qh, start=True, stop=False)
                        t.matmul(pS[c][:, r], kh, ql, start=False, stop=False)
                        mm = t.matmul(pS[c][:, r], kl, qh, start=False, stop=True)
                    pe.inc(mm)

                @block.vector
                def _(v, kc=kc, q0=q0):
                    v.wait_ge(pe.sem, pe.n)
                    v.wait_ge(ge.sem, ge.n)
                    c0 = q0 // 512
                    r = slice(q0 - c0 * 512, q0 - c0 * 512 + 128)
                    msl = smask[:, kc * 128:(kc + 1) * 128]
                    v.tensor_tensor(pS[c0][:, r], pS[c0][:, r], msl, ALU.add)
                    ve.inc(v.drain())

                @block.scalar
                def _(a, kc=kc, chunks=chunks):
                    a.wait_ge(ve.sem, ve.n)
                    for (c, lo_c, hi_c) in chunks:
                        r = slice(lo_c - c * 512, hi_c - c * 512)
                        inst = a.activation(pf[:, lo_c:hi_c], pS[c][:, r], ACT.Exp,
                                            bias=expb[:, 0:1], scale=exps[:, 0:1])
                    ae.inc(inst)

                @block.vector
                def _(v, kc=kc, q0=q0):
                    v.wait_ge(ae.sem, ae.n)
                    v.tensor_copy(phi[:, q0:TPC], pf[:, q0:TPC])
                    v.drain()
                    v.tensor_sub(plo[:, q0:TPC], pf[:, q0:TPC], phi[:, q0:TPC])
                    ve.inc(v.drain())

                @block.tensor
                def _(t, h=h, kc=kc, chunks=chunks):
                    t.wait_ge(ve.sem, ve.n)
                    vh = sv["hi"][:, kc * 128:(kc + 1) * 128]
                    vl = sv["lo"][:, kc * 128:(kc + 1) * 128]
                    first = kc == 0
                    for (c, lo_c, hi_c) in chunks:
                        r = slice(lo_c - c * 512, hi_c - c * 512)
                        ph_ = phi[:, lo_c:hi_c]
                        pl_ = plo[:, lo_c:hi_c]
                        t.matmul(pO[c][:, r], vh, ph_, start=first, stop=False)
                        t.matmul(pO[c][:, r], vh, pl_, start=False, stop=False)
                        t.matmul(pO[c][:, r], vl, ph_, start=False, stop=False)
                        t.matmul(pZ[c][:, r], onesk[:, :], ph_, start=first, stop=False)
                        mm = t.matmul(pZ[c][:, r], onesk[:, :], pl_, start=False,
                                      stop=False)
                    pe.inc(mm)

            @block.tensor
            def _(t, h=h):
                t.matmul(pO[0][0:1, 0:1], zerok[:, 0:1], zerok[:, 0:1],
                         start=False, stop=True)
                t.matmul(pO[1][0:1, 0:1], zerok[:, 0:1], zerok[:, 0:1],
                         start=False, stop=True)
                t.matmul(pZ[0][0:1, 0:1], zerok[:, 0:1], zerok[:, 0:1],
                         start=False, stop=True)
                mm = t.matmul(pZ[1][0:1, 0:1], zerok[:, 0:1], zerok[:, 0:1],
                              start=False, stop=True)
                pe.inc(mm)

            @block.vector
            def _(v):
                v.wait_ge(pe.sem, pe.n)
                v.reciprocal(zr[0:1, 0:512], pZ[0][0:1, :])
                v.reciprocal(zr[0:1, 512:1024], pZ[1][0:1, :])
                ve.inc(v.drain())

            @block.scalar
            def _(a):
                a.wait_ge(pe.sem, pe.n)
                a.copy(osb[:, 0:512], pO[0][:, :])
                ae.inc(a.copy(osb[:, 512:1024], pO[1][:, :]))

            @block.tensor
            def _(t):
                t.wait_ge(ve.sem, ve.n)
                for tt in range(NT):
                    tr = t.transpose(pzt[:, tt:tt + 1], zr[0:1, tt * 128:(tt + 1) * 128],
                                     identf[0:1, 0:1])
                pe.inc(tr)

            @block.scalar
            def _(a):
                a.wait_ge(pe.sem, pe.n)
                ae.inc(a.copy(zrs[:, :], pzt[:, :]))

            for half in range(2):
                @block.tensor
                def _(t, half=half):
                    t.wait_ge(ae.sem, ae.n)
                    for i in range(4):
                        tt = half * 4 + i
                        tr = t.transpose(ptm[:, i * 128:(i + 1) * 128],
                                         osb[:, tt * 128:(tt + 1) * 128], identf[:])
                    pe.inc(tr)

                @block.scalar
                def _(a, half=half):
                    a.wait_ge(pe.sem, pe.n)
                    ae.inc(a.copy(otm[:, half * 512:(half + 1) * 512], ptm[:, :]))

            @block.vector
            def _(v):
                v.wait_ge(ae.sem, ae.n)
                for tt in range(NT):
                    v.tensor_single_scalar(otm[:, tt * 128:(tt + 1) * 128],
                                           otm[:, tt * 128:(tt + 1) * 128],
                                           zrs[:, tt:tt + 1], ALU.mult)
                ve.inc(v.drain())

            @block.gpsimd
            def _(g, h=h):
                g.wait_ge(ve.sem, ve.n)
                for tt in range(NT):
                    st.inc(g.dma_start(
                        out=o_out[tt * 128:(tt + 1) * 128, h * 128:(h + 1) * 128],
                        in_=otm[:, tt * 128:(tt + 1) * 128]), 16)

        @block.gpsimd
        def _(g):
            g.wait_ge(st.sem, st.n)
    return nc


def host_masks(c):
    out = np.zeros((32, 128, 128), dtype=np.float32)
    tiles = seq_tiles(c)
    for kc in range(32):
        p = tiles[l2_jmin(kc)]
        if p > kc:
            continue
        kg = kc * 128 + np.arange(128)[:, None]
        qg = p * 128 + np.arange(128)[None, :]
        out[kc] = np.where(qg >= kg, 0.0, NEG_BIG).astype(np.float32)
    return out



def build_l3(dbg=False):
    nc = bass.Bass("TRN2", target_bir_lowering=False)
    o_in = nc.dram_tensor("o", [TPC, D], F32, kind="ExternalInput").ap()
    x_in = nc.dram_tensor("x", [TPC, D], F32, kind="ExternalInput").ap()
    ident_in = nc.dram_tensor("ident", [128, 128], BF16, kind="ExternalInput").ap()
    identf_in = nc.dram_tensor("identf", [128, 128], F32, kind="ExternalInput").ap()
    bias_in = nc.dram_tensor("bias", [128, 32], F32, kind="ExternalInput").ap()
    w_in = {}
    for m in ("wo", "ff"):
        for p in ("hi", "lo"):
            w_in[m + p] = nc.dram_tensor(m + "_" + p, [8, 32, 128, 512], BF16,
                                         kind="ExternalInput").ap()
    y_out = nc.dram_tensor("y", [TPC, D], F32, kind="ExternalOutput").ap()
    knd = "ExternalOutput" if dbg else "Internal"
    aT = nc.dram_tensor("aT", [TPC, D], F32, kind=knd).ap()
    fT = nc.dram_tensor("fT", [TPC, D], F32, kind=knd).ap()
    x1r = nc.dram_tensor("x1r", [TPC, D], F32, kind=knd).ap()

    xs = nc.alloc_sbuf_tensor("xs", [128, D], F32).ap()
    xt2 = nc.alloc_sbuf_tensor("xt2", [128, D], F32).ap()
    xhi = nc.alloc_sbuf_tensor("xhi", [128, D], BF16).ap()
    xlo = nc.alloc_sbuf_tensor("xlo", [128, D], BF16).ap()
    xqhi = nc.alloc_sbuf_tensor("xqhi", [128, NKC * TPC], BF16).ap()
    xqlo = nc.alloc_sbuf_tensor("xqlo", [128, NKC * TPC], BF16).ap()
    ident = nc.alloc_sbuf_tensor("identt", [128, 128], BF16).ap()
    identf = nc.alloc_sbuf_tensor("identtf", [128, 128], F32).ap()
    sbias = nc.alloc_sbuf_tensor("sbias", [128, 32], F32).ap()
    stage2 = nc.alloc_sbuf_tensor("stage2", [128, 512], F32).ap()
    scr = alloc_quant_scratch(nc)
    wbuf = {(p, b): nc.alloc_sbuf_tensor(f"wbuf_{p}{b}", [128, 512], BF16).ap()
            for p in ("hi", "lo") for b in range(2)}
    stage = [nc.alloc_sbuf_tensor(f"stg{i}", [128, 512], F32).ap() for i in range(4)]
    lm = nc.alloc_sbuf_tensor("lm", [128, 1], F32).ap()
    lv = nc.alloc_sbuf_tensor("lv", [128, 1], F32).ap()
    lsd = nc.alloc_sbuf_tensor("lsd", [128, 1], F32).ap()
    lrs = nc.alloc_sbuf_tensor("lrs", [128, 1], F32).ap()
    ln_s = nc.alloc_sbuf_tensor("ln_s", [128, 1], F32).ap()
    ln_b = nc.alloc_sbuf_tensor("ln_b", [128, 1], F32).ap()

    with ExitStack() as ctx:
        psum = [ctx.enter_context(nc.psum_tensor(f"pj{i}", [128, 512], F32))
                for i in range(4)]
        pt = ctx.enter_context(nc.psum_tensor("pt", [128, 512], BF16))
        ptf = ctx.enter_context(nc.psum_tensor("ptf", [128, 512], F32))
        ld = Sem(ctx.enter_context(nc.semaphore("ld")))
        wl = Sem(ctx.enter_context(nc.semaphore("wl")))
        ve = Sem(ctx.enter_context(nc.semaphore("ve")))
        pe = Sem(ctx.enter_context(nc.semaphore("pe")))
        pw = Sem(ctx.enter_context(nc.semaphore("pw")))
        ae = Sem(ctx.enter_context(nc.semaphore("ae")))
        ge = Sem(ctx.enter_context(nc.semaphore("ge")))
        st = Sem(ctx.enter_context(nc.semaphore("st")))
        block = ctx.enter_context(nc.Block())

        @block.sync
        def _(s):
            ld.inc(s.dma_start(out=ident[:], in_=ident_in[:, :]), 16)
            ld.inc(s.dma_start(out=identf[:], in_=identf_in[:, :]), 16)
            ld.inc(s.dma_start(out=sbias[:], in_=bias_in[:, :]), 16)

        @block.gpsimd
        def _(g):
            g.memset(ln_s[:, :], 1.0 / 4096.0)
            g.memset(ln_b[:, :], LN_EPS)
            ge.inc(g.engine_nop())

        def quant_split_transpose(src_dram, row_base=0):
            for tt in range(NT):
                @block.sync
                def _(s, tt=tt):
                    s.wait_ge(ve.sem, ve.n)
                    ld.inc(s.dma_start(out=xs[:],
                                       in_=src_dram[tt * 128:(tt + 1) * 128, :]), 16)

                @block.vector
                def _(v, tt=tt):
                    v.wait_ge(ld.sem, ld.n)
                    v.wait_ge(ae.sem, ae.n)
                    emit_quant(v, xs, xs, scr)
                    v.tensor_copy(xhi[:], xs[:])
                    v.drain()
                    v.tensor_sub(xlo[:], xs[:], xhi[:])
                    ve.inc(v.drain())

                for kcg in range(8):
                    for part, buf, dstq in (("hi", xhi, xqhi), ("lo", xlo, xqlo)):
                        @block.tensor
                        def _(t, kcg=kcg, buf=buf):
                            t.wait_ge(ve.sem, ve.n)
                            t.wait_ge(ae.sem, ae.n)
                            for i in range(4):
                                kc = kcg * 4 + i
                                tr = t.transpose(pt[:, i * 128:(i + 1) * 128],
                                                 buf[:, kc * 128:(kc + 1) * 128],
                                                 ident[:])
                            pe.inc(tr)

                        @block.scalar
                        def _(a, tt=tt, kcg=kcg, dstq=dstq):
                            a.wait_ge(pe.sem, pe.n)
                            dst3 = dstq[:, :].rearrange("p (kc t) -> p kc t", t=TPC)
                            dst_sl = dst3[:, kcg * 4:(kcg + 1) * 4,
                                          tt * 128:(tt + 1) * 128]
                            ae.inc(a.copy(dst_sl,
                                          pt[:, :].rearrange("p (i q) -> p i q", q=128)))

        def projection(pi, whi_d, wlo_d, dst_dram, with_bias):
            for tc in range(2):
                for nbi in range(8):
                    @block.sync
                    def _(s, pi=pi, tc=tc, nbi=nbi, whi_d=whi_d, wlo_d=wlo_d):
                        for kc in range(32):
                            slab = ((pi * 2 + tc) * 8 + nbi) * 32 + kc
                            if slab >= 2:
                                s.wait_ge(pw.sem, slab - 1)
                            b = slab % 2
                            wl.inc(s.dma_start(out=wbuf[("hi", b)][:],
                                               in_=whi_d[nbi, kc, :, :]), 16)
                            wl.inc(s.dma_start(out=wbuf[("lo", b)][:],
                                               in_=wlo_d[nbi, kc, :, :]), 16)

                    @block.tensor
                    def _(t, pi=pi, tc=tc, nbi=nbi):
                        t.wait_ge(ae.sem, ae.n)
                        for kc in range(32):
                            slab = ((pi * 2 + tc) * 8 + nbi) * 32 + kc
                            t.wait_ge(wl.sem, 64 if slab <= 1 else 32 * (slab + 1))
                            b = slab % 2
                            first = kc == 0
                            last = kc == 31
                            for nc4 in range(4):
                                mv = xqhi[:, kc * TPC + tc * 512:
                                          kc * TPC + tc * 512 + 512]
                                mvl = xqlo[:, kc * TPC + tc * 512:
                                           kc * TPC + tc * 512 + 512]
                                wh = wbuf[("hi", b)][:, nc4 * 128:(nc4 + 1) * 128]
                                wll = wbuf[("lo", b)][:, nc4 * 128:(nc4 + 1) * 128]
                                t.matmul(psum[nc4][:], wh, mv, start=first, stop=False)
                                t.matmul(psum[nc4][:], wh, mvl, start=False, stop=False)
                                mm = t.matmul(psum[nc4][:], wll, mv, start=False,
                                              stop=last)
                            pw.inc(mm)

                    @block.scalar
                    def _(a, tc=tc, nbi=nbi):
                        a.wait_ge(pw.sem, pw.n)
                        a.wait_ge(ve.sem, ve.n)
                        a.wait_ge(pe.sem, pe.n)
                        for nc4 in range(4):
                            ae.inc(a.copy(stage[nc4][:], psum[nc4][:]))

                    if with_bias:
                        @block.vector
                        def _(v, nbi=nbi):
                            v.wait_ge(ae.sem, ae.n)
                            for nc4 in range(4):
                                idx = 4 * nbi + nc4
                                v.tensor_single_scalar(stage[nc4][:], stage[nc4][:],
                                                       sbias[:, idx:idx + 1], ALU.add)
                            ve.inc(v.drain())

                    for nc4 in range(4):
                        @block.tensor
                        def _(t, nc4=nc4):
                            t.wait_ge(ae.sem, ae.n)
                            t.wait_ge(ve.sem, ve.n)
                            for i in range(4):
                                tr = t.transpose(ptf[:, i * 128:(i + 1) * 128],
                                                 stage[nc4][:, i * 128:(i + 1) * 128],
                                                 identf[:])
                            pe.inc(tr)

                        @block.scalar
                        def _(a, nc4=nc4):
                            a.wait_ge(pe.sem, pe.n)
                            a.wait_ge(st.sem, st.n)
                            ae.inc(a.copy(stage2[:], ptf[:, :]))

                        @block.gpsimd
                        def _(g, tc=tc, nbi=nbi, nc4=nc4, dst_dram=dst_dram):
                            g.wait_ge(ae.sem, ae.n)
                            col0 = (4 * nbi + nc4) * 128
                            for i in range(4):
                                row0 = tc * 512 + i * 128
                                st.inc(g.dma_start(
                                    out=dst_dram[row0:row0 + 128, col0:col0 + 128],
                                    in_=stage2[:, i * 128:(i + 1) * 128]), 16)

        def ln_pass(a_dram_T, res_dram, dst_rows, also_split):
            for tt in range(NT):
                @block.sync
                def _(s, tt=tt):
                    s.wait_ge(ve.sem, ve.n)
                    s.wait_ge(st.sem, st.n)
                    ld.inc(s.dma_start(
                        out=xs[:, :], in_=a_dram_T[tt * 128:(tt + 1) * 128, :]), 16)
                    ld.inc(s.dma_start(out=xt2[:],
                                       in_=res_dram[tt * 128:(tt + 1) * 128, :]), 16)

                @block.vector
                def _(v, tt=tt):
                    v.wait_ge(ld.sem, ld.n)
                    v.wait_ge(ae.sem, ae.n)
                    v.tensor_add(xs[:], xs[:], xt2[:])
                    v.drain()
                    v.tensor_reduce(lm[:, :], xs[:], AX.X, ALU.add)
                    v.drain()
                    v.tensor_single_scalar(lm[:, :], lm[:, :], 1.0 / 4096.0, ALU.mult)
                    v.drain()
                    v.tensor_single_scalar(xs[:], xs[:], lm[:, 0:1], ALU.subtract)
                    v.drain()
                    for cchunk in range(4):
                        sl = slice(cchunk * 1024, (cchunk + 1) * 1024)
                        v.tensor_mul(scr["m0"][:, :], xs[:, sl], xs[:, sl])
                        v.drain()
                        v.tensor_reduce(scr["gm"][:, 0:1], scr["m0"][:, :], AX.X, ALU.add)
                        v.drain()
                        if cchunk == 0:
                            v.tensor_copy(lv[:, :], scr["gm"][:, 0:1])
                        else:
                            v.tensor_add(lv[:, :], lv[:, :], scr["gm"][:, 0:1])
                        v.drain()
                    ve.inc(v.drain())

                @block.scalar
                def _(a):
                    a.wait_ge(ve.sem, ve.n)
                    ae.inc(a.activation(lsd[:, :], lv[:, :], ACT.Sqrt,
                                        bias=ln_b[:, 0:1], scale=ln_s[:, 0:1]))

                @block.vector
                def _(v, tt=tt):
                    v.wait_ge(ae.sem, ae.n)
                    v.reciprocal(lrs[:, :], lsd[:, :])
                    v.drain()
                    v.tensor_single_scalar(xs[:], xs[:], lrs[:, 0:1], ALU.mult)
                    ve.inc(v.drain())

                @block.gpsimd
                def _(g, tt=tt):
                    g.wait_ge(ve.sem, ve.n)
                    st.inc(g.dma_start(out=dst_rows[tt * 128:(tt + 1) * 128, :],
                                       in_=xs[:]), 16)

                if also_split:
                    @block.vector
                    def _(v, tt=tt):
                        v.tensor_copy(xhi[:], xs[:])
                        v.drain()
                        v.tensor_sub(xlo[:], xs[:], xhi[:])
                        ve.inc(v.drain())

                    for kcg in range(8):
                        for part, buf, dstq in (("hi", xhi, xqhi), ("lo", xlo, xqlo)):
                            @block.tensor
                            def _(t, kcg=kcg, buf=buf):
                                t.wait_ge(ve.sem, ve.n)
                                t.wait_ge(ae.sem, ae.n)
                                for i in range(4):
                                    kc = kcg * 4 + i
                                    tr = t.transpose(pt[:, i * 128:(i + 1) * 128],
                                                     buf[:, kc * 128:(kc + 1) * 128],
                                                     ident[:])
                                pe.inc(tr)

                            @block.scalar
                            def _(a, tt=tt, kcg=kcg, dstq=dstq):
                                a.wait_ge(pe.sem, pe.n)
                                dst3 = dstq[:, :].rearrange("p (kc t) -> p kc t", t=TPC)
                                dst_sl = dst3[:, kcg * 4:(kcg + 1) * 4,
                                              tt * 128:(tt + 1) * 128]
                                ae.inc(a.copy(
                                    dst_sl,
                                    pt[:, :].rearrange("p (i q) -> p i q", q=128)))

        quant_split_transpose(o_in)
        projection(0, w_in["wohi"], w_in["wolo"], aT, with_bias=False)
        ln_pass(aT, x_in, x1r, also_split=True)
        projection(1, w_in["ffhi"], w_in["fflo"], fT, with_bias=True)
        ln_pass(fT, x1r, y_out, also_split=False)

        @block.gpsimd
        def _(g):
            g.wait_ge(st.sem, st.n)
    return nc



def kernel(L0_attn_hidden_states, L0_attn_Wq, L0_attn_Wk, L0_attn_Wv, L0_attn_Wo,
           L0_ffn_qweight, L0_ffn_scales, L0_ffn_scaled_zeros, L0_ffn_bias):
    f = np.float32
    cores = list(range(NCORE))
    x_full = np.ascontiguousarray(L0_attn_hidden_states.reshape(B * S, D), dtype=f)
    ident = np.eye(128, dtype=ml_dtypes.bfloat16)
    identf = np.eye(128, dtype=f)
    perm = host_ab_perm()
    Wq_p = np.ascontiguousarray(L0_attn_Wq[perm])
    Wk_p = np.ascontiguousarray(L0_attn_Wk[perm])
    fq = np.ascontiguousarray(L0_ffn_qweight.reshape(D, D), dtype=np.int32)
    fs = np.ascontiguousarray(L0_ffn_scales.reshape(D, 32), dtype=f)
    fz = np.ascontiguousarray(L0_ffn_scaled_zeros.reshape(D, 32), dtype=f)

    in0 = []
    for c in cores:
        sl = slice(c * 512, (c + 1) * 512)
        in0.append({"wq": Wq_p[sl], "wk": Wk_p[sl],
                    "wv": np.ascontiguousarray(L0_attn_Wv[sl]),
                    "wo": np.ascontiguousarray(L0_attn_Wo[sl]),
                    "fq": fq[sl], "fs": fs[sl], "fz": fz[sl], "ident": ident})
    r0 = run_bass_kernel_spmd(build_l0(), in0, core_ids=cores).results

    blocked = {}
    for m, ab in (("wq", True), ("wk", True), ("wv", False), ("wo", False), ("ff", False)):
        hi = np.concatenate([r0[c][m + "T_hi"] for c in cores], axis=2).reshape(D, D)
        lo = np.concatenate([r0[c][m + "T_lo"] for c in cores], axis=2).reshape(D, D)
        blocked[m] = host_block_w(hi, lo, ab)

    in1 = []
    for c in cores:
        rows = np.concatenate([np.arange(t * 128, t * 128 + 128) for t in core_tiles(c)])
        in1.append({"x": np.ascontiguousarray(x_full[rows]),
                    "pos": np.tile(host_positions(c), (128, 1)),
                    "ident": ident,
                    "wq_hi": blocked["wq"][0], "wq_lo": blocked["wq"][1],
                    "wk_hi": blocked["wk"][0], "wk_lo": blocked["wk"][1],
                    "wv_hi": blocked["wv"][0], "wv_lo": blocked["wv"][1]})
    r1 = run_bass_kernel_spmd(build_l1(), in1, core_ids=cores).results

    kT = {p: [] for p in ("hi", "lo")}
    vv = {p: [] for p in ("hi", "lo")}
    for b in range(B):
        for p in ("hi", "lo"):
            kb = np.empty((32, 128, S), dtype=ml_dtypes.bfloat16)
            vb = np.empty((32, 32, 128, 128), dtype=ml_dtypes.bfloat16)
            for pt in range(32):
                c = 4 * b + pt % 4
                j = pt // 4
                kb[:, :, pt * 128:(pt + 1) * 128] = r1[c]["kT_" + p][:, :, j * 128:(j + 1) * 128]
                vb[:, pt, :, :] = r1[c]["v_" + p][:, j, :, :]
            kT[p].append(kb)
            vv[p].append(vb)

    in2 = []
    for c in cores:
        b = c // 4
        in2.append({"qT_hi": r1[c]["qT_hi"], "qT_lo": r1[c]["qT_lo"],
                    "kT_hi": kT["hi"][b], "kT_lo": kT["lo"][b],
                    "v_hi": vv["hi"][b], "v_lo": vv["lo"][b],
                    "masks": host_masks(c), "ident": ident, "identf": identf})
    r2 = run_bass_kernel_spmd(build_l2(), in2, core_ids=cores).results

    bias_l3 = np.ascontiguousarray(np.asarray(L0_ffn_bias, dtype=f).reshape(32, 128).T)
    in3 = []
    for c in cores:
        rows = np.concatenate([np.arange(t * 128, t * 128 + 128) for t in core_tiles(c)])
        in3.append({"o": r2[c]["o"], "x": np.ascontiguousarray(x_full[rows]),
                    "ident": ident, "identf": identf, "bias": bias_l3,
                    "wo_hi": blocked["wo"][0], "wo_lo": blocked["wo"][1],
                    "ff_hi": blocked["ff"][0], "ff_lo": blocked["ff"][1]})
    r3 = run_bass_kernel_spmd(build_l3(), in3, core_ids=cores).results

    out = np.empty((B * S, D), dtype=f)
    for c in cores:
        rows = np.concatenate([np.arange(t * 128, t * 128 + 128) for t in core_tiles(c)])
        out[rows] = r3[c]["y"]
    return out.reshape(B, S, D)
